# version 03
# baseline (speedup 1.0000x reference)
from contextlib import ExitStack
import numpy as np
import ml_dtypes
import concourse.bass as bass
import concourse.mybir as mybir
from concourse.bass_utils import run_bass_kernel_spmd

F32 = mybir.dt.float32
BF16 = mybir.dt.bfloat16
AF = mybir.ActivationFunctionType
ALU = mybir.AluOpType
AX = mybir.AxisListType
BF = ml_dtypes.bfloat16

NCORES = 8
B = 8
SQ = 2048
DM = 1024
NT = SQ // 128


class Ctx:
    def __init__(self, name):
        self.nc = bass.Bass("TRN2", target_bir_lowering=False, debug=False)
        self.name = name
        self._sems = {}
        self.c = {}
        self.es = ExitStack()

    def sb(self, name, shape, dtype):
        t = self.es.enter_context(self.nc.sbuf_tensor(name, shape, dtype))
        import math as _m
        nbytes = int(np.prod(shape[1:])) * mybir.dt.size(dtype)
        if nbytes % 64 != 0 and (nbytes % 64) <= 32:
            if not hasattr(self, "_nfill"):
                self._nfill = 0
            self._nfill += 1
            self.es.enter_context(self.nc.sbuf_tensor(f"_fill{self._nfill}", [128, 1], F32))
        return t

    def ps(self, name, shape, dtype=F32):
        return self.es.enter_context(self.nc.psum_tensor(name, shape, dtype))

    def close(self):
        self.es.close()

    def constcol(self, val, dtype=F32):
        key = (float(val), dtype)
        if not hasattr(self, "_cc"):
            self._cc = {}
        if key not in self._cc:
            t = self.sb(f"cc{len(self._cc)}", [128, 1], dtype)
            self.nc.vector.memset(t[:, :], float(val))
            self._cc[key] = t
        return self._cc[key]

    def sem(self, name):
        if name not in self._sems:
            self._sems[name] = self.nc.alloc_semaphore(name=name)
            self.c[name] = 0
        return self._sems[name]

    def inc(self, instr, name, amt=1):
        instr.then_inc(self.sem(name), amt)
        self.c[name] += amt
        return self.c[name]

    def ch(self, eng, name="p"):
        self.wait(eng, name)
        return lambda instr: self.inc(instr, name)

    def n(self, name):
        self.sem(name)
        return self.c[name]

    def wait(self, eng, name, val=None):
        self.sem(name)
        if val is None:
            val = self.c[name]
        if val > 0:
            eng.wait_ge(self._sems[name], val)

    def bar(self):
        self.nc.all_engine_barrier()

    def drain(self):
        for name, sem in self._sems.items():
            if self.c[name] > 0:
                self.nc.sync.wait_ge(sem, self.c[name])
        self.nc.all_engine_barrier()


def preload_set(nc, setname):
    from concourse.hw_specs import get_activation_tables
    names = list(get_activation_tables(nc.m.arch).keys())
    idx = names.index(setname)
    return nc.scalar.add_instruction(mybir.InstLoadActFuncSet(
        name=nc.get_next_instruction_name(), act_func_set_id=idx, ins=[], outs=[]))


def emit_ln(ctx, xt, P, D, xn_out, scr, s_ffn=None, eps=1e-5):
    nc = ctx.nc
    v, a = nc.vector, nc.scalar
    mu, varsum, rstd, cen, sq = scr["mu"], scr["varsum"], scr["rstd"], scr["cen"], scr["sq"]
    t1, t2 = scr["t1"], scr["t2"]
    epst = ctx.constcol(eps)
    ctx.ch(v)(v.reduce_sum(t1[0:P, 0:1], xt[0:P, 0:D], axis=AX.X))
    ctx.ch(v)(v.tensor_scalar_mul(mu[0:P, 0:1], t1[0:P, 0:1], 1.0 / D))
    ctx.ch(v)(v.tensor_scalar(cen[0:P, 0:D], xt[0:P, 0:D], mu[0:P, 0:1], None, op0=ALU.subtract))
    ctx.ch(a)(a.activation(out=sq[0:P, 0:D], in_=cen[0:P, 0:D], func=AF.Square))
    ctx.ch(v)(v.reduce_sum(varsum[0:P, 0:1], sq[0:P, 0:D], axis=AX.X))
    ctx.ch(a)(a.activation(out=t1[0:P, 0:1], in_=varsum[0:P, 0:1], func=AF.Sqrt,
                           scale=1.0 / D, bias=epst[0:P, 0:1]))
    ctx.ch(v)(v.reciprocal(rstd[0:P, 0:1], t1[0:P, 0:1]))
    n = ctx.ch(v)(v.tensor_scalar(xn_out[0:P, 0:D], cen[0:P, 0:D], rstd[0:P, 0:1], None, op0=ALU.mult))
    if s_ffn is not None:
        ctx.ch(v)(v.tensor_mul(t1[0:P, 0:1], rstd[0:P, 0:1], rstd[0:P, 0:1]))
        ctx.ch(v)(v.tensor_mul(t2[0:P, 0:1], t1[0:P, 0:1], varsum[0:P, 0:1]))
        ctx.ch(a)(a.activation(out=t1[0:P, 0:1], in_=t2[0:P, 0:1], func=AF.Sqrt, scale=1.0 / D))
        ctx.ch(v)(v.tensor_scalar_add(t2[0:P, 0:1], t1[0:P, 0:1], 1e-8))
        n = ctx.ch(v)(v.reciprocal(s_ffn[0:P, 0:1], t2[0:P, 0:1]))
    return n


def build_l0a():
    ctx = Ctx("l0a")
    nc = ctx.nc
    x = nc.dram_tensor("x", [SQ, DM], F32, kind="ExternalInput").ap()
    wq = nc.dram_tensor("wq", [DM, DM], F32, kind="ExternalInput").ap()
    wk = nc.dram_tensor("wk", [DM, DM], F32, kind="ExternalInput").ap()
    wv = nc.dram_tensor("wv", [DM, DM], F32, kind="ExternalInput").ap()
    ident = nc.dram_tensor("ident", [128, 128], BF16, kind="ExternalInput").ap()
    xnT_o = nc.dram_tensor("xnT", [DM, SQ], BF16, kind="ExternalOutput").ap()
    q_o = nc.dram_tensor("q", [SQ, DM], BF16, kind="ExternalOutput").ap()
    k_o = nc.dram_tensor("k", [SQ, DM], BF16, kind="ExternalOutput").ap()
    v_o = nc.dram_tensor("v", [SQ, DM], BF16, kind="ExternalOutput").ap()
    qT_o = nc.dram_tensor("qT", [32, 8, 128, 64], BF16, kind="ExternalOutput").ap()
    kT_o = nc.dram_tensor("kT", [32, 8, 128, 64], BF16, kind="ExternalOutput").ap()
    s_o = nc.dram_tensor("s_ffn", [1, SQ], F32, kind="ExternalOutput").ap()

    if True:
        wqt = ctx.sb("wqt", [128, 8 * DM], BF16)
        wkt = ctx.sb("wkt", [128, 8 * DM], BF16)
        wvt = ctx.sb("wvt", [128, 8 * DM], BF16)
        idt = ctx.sb("idt", [128, 128], BF16)
        xtb = [ctx.sb("xt0", [128, DM], F32), ctx.sb("xt1", [128, DM], F32)]
        xn = ctx.sb("xn", [128, DM], BF16)
        xnT = ctx.sb("xnT_s", [128, DM], BF16)
        qkv = ctx.sb("qkv", [128, 3 * DM], BF16)
        tT = ctx.sb("tT", [128, 2 * DM], BF16)
        mu = ctx.sb("mu", [128, 1], F32)
        varsum = ctx.sb("varsum", [128, 1], F32)
        rstd = ctx.sb("rstd", [128, 1], F32)
        cen = ctx.sb("cen", [128, DM], F32)
        sqs = ctx.sb("sqs", [128, DM], F32)
        t1 = ctx.sb("t1", [128, 1], F32)
        t2 = ctx.sb("t2", [128, 1], F32)
        sff = ctx.sb("sff", [128, 1], F32)
        psA = ctx.ps("psA", [128, 512], F32)
        psB = ctx.ps("psB", [128, 512], F32)
        psT = ctx.ps("psT", [128, 128], BF16)
        psT2 = ctx.ps("psT2", [128, 128], BF16)
        scr = {"mu": mu, "varsum": varsum, "rstd": rstd, "cen": cen, "sq": sqs, "t1": t1, "t2": t2}
        v, a, te, sy, gp = nc.vector, nc.scalar, nc.tensor, nc.sync, nc.gpsimd
        for w_hbm, w_sb in ((wq, wqt), (wk, wkt), (wv, wvt)):
            for kt in range(8):
                ctx.inc(gp.dma_start(out=w_sb[:, kt * DM:(kt + 1) * DM],
                                     in_=w_hbm[kt * 128:(kt + 1) * 128, :]), "gld", 16)
        ctx.inc(sy.dma_start(out=idt[:, :], in_=ident[:, :]), "wld", 16)
        ctx.inc(sy.dma_start(out=xtb[0][:, :], in_=x[0:128, :]), "xld0", 16)
        ctx.constcol(1e-5)
        ctx.wait(sy, "wld")
        ctx.wait(sy, "xld0")
        ctx.wait(gp, "gld")
        ctx.bar()

        for it in range(NT):
            bi = it % 2
            t0 = it * 128
            xt = xtb[bi]
            if it + 1 < NT:
                ctx.inc(sy.dma_start(out=xtb[1 - bi][:, :], in_=x[t0 + 128:t0 + 256, :]), f"xld{1 - bi}", 16)
            ctx.wait(v, f"xld{bi}", 16 * (it // 2 + 1))
            n = emit_ln(ctx, xt, 128, DM, xn, scr, s_ffn=sff, eps=1e-5)
            ctx.wait(sy, "p", n)
            ctx.inc(sy.dma_start(out=s_o[0:1, t0:t0 + 128].rearrange("o p -> p o"),
                                 in_=sff[:, 0:1]), "st", 16)
            ln_done = ctx.n("p")
            ctx.wait(te, "p", ln_done)
            for ib in range(8):
                ps = psT if ib % 2 == 0 else psT2
                ctx.wait(te, "tpe", ctx.n("tpe") - 1)
                ctx.inc(te.transpose(ps[:, :], xn[:, ib * 128:(ib + 1) * 128], idt[:, :]), "tp")
                ctx.wait(a, "tp", ctx.n("tp"))
                ctx.inc(a.activation(out=xnT[:, ib * 128:(ib + 1) * 128], in_=ps[:, :],
                                     func=AF.Identity), "tpe")
            ctx.wait(sy, "tpe", ctx.n("tpe"))
            for ib in range(8):
                ctx.inc(sy.dma_start(out=xnT_o[ib * 128:(ib + 1) * 128, t0:t0 + 128],
                                     in_=xnT[:, ib * 128:(ib + 1) * 128]), "st", 16)
            ctx.wait(te, "tpe", ctx.n("tpe"))
            for wi, wsb in enumerate((wqt, wkt, wvt)):
                for oc in range(2):
                    ps = psA if oc == 0 else psB
                    ctx.wait(te, "ev", ctx.n("ev") - 1)
                    for kt in range(8):
                        mm = te.matmul(ps[:, :], xnT[:, kt * 128:(kt + 1) * 128],
                                       wsb[:, kt * DM + oc * 512: kt * DM + (oc + 1) * 512],
                                       start=(kt == 0), stop=(kt == 7))
                    ctx.inc(mm, "mm")
                    ctx.wait(v, "mm", ctx.n("mm"))
                    ctx.inc(v.tensor_copy(qkv[:, wi * DM + oc * 512: wi * DM + (oc + 1) * 512],
                                          ps[:, :]), "ev")
            ctx.wait(sy, "ev", ctx.n("ev"))
            ctx.inc(sy.dma_start(out=q_o[t0:t0 + 128, :], in_=qkv[:, 0:DM]), "st", 16)
            ctx.inc(sy.dma_start(out=k_o[t0:t0 + 128, :], in_=qkv[:, DM:2 * DM]), "st", 16)
            ctx.inc(sy.dma_start(out=v_o[t0:t0 + 128, :], in_=qkv[:, 2 * DM:3 * DM]), "st", 16)
            ctx.wait(te, "ev", ctx.n("ev"))
            for wi in range(2):
                for ib in range(8):
                    ps = psT if ib % 2 == 0 else psT2
                    ctx.wait(te, "tpe", ctx.n("tpe") - 1)
                    ctx.inc(te.transpose(ps[:, :], qkv[:, wi * DM + ib * 128: wi * DM + (ib + 1) * 128],
                                         idt[:, :]), "tp")
                    ctx.wait(a, "tp", ctx.n("tp"))
                    ctx.inc(a.activation(out=tT[:, wi * DM + ib * 128: wi * DM + (ib + 1) * 128],
                                         in_=ps[:, :], func=AF.Identity), "tpe")
            ctx.wait(sy, "tpe", ctx.n("tpe"))
            for wi, dst in ((0, qT_o), (1, kT_o)):
                for ib in range(8):
                    ctx.inc(sy.dma_start(out=dst[2 * it, ib, :, :],
                                         in_=tT[:, wi * DM + ib * 128: wi * DM + ib * 128 + 64]), "st", 16)
                    ctx.inc(sy.dma_start(out=dst[2 * it + 1, ib, :, :],
                                         in_=tT[:, wi * DM + ib * 128 + 64: wi * DM + (ib + 1) * 128]), "st", 16)
            ctx.wait(sy, "st")
            ctx.bar()
        ctx.drain()
    ctx.close()
    return nc




def build_l0b(NCH=32, STAGE=9, DBG=False, BCUT=-1):
    ctx = Ctx("l0b")
    nc = ctx.nc
    q = nc.dram_tensor("q", [SQ, DM], BF16, kind="ExternalInput").ap()
    k = nc.dram_tensor("k", [SQ, DM], BF16, kind="ExternalInput").ap()
    v_in = nc.dram_tensor("v", [SQ, DM], BF16, kind="ExternalInput").ap()
    qT = nc.dram_tensor("qT", [32, 8, 128, 64], BF16, kind="ExternalInput").ap()
    kT = nc.dram_tensor("kT", [32, 8, 128, 64], BF16, kind="ExternalInput").ap()
    W1st = nc.dram_tensor("W1st", [128, 512], F32, kind="ExternalInput").ap()
    binit = nc.dram_tensor("binit", [128, DM], F32, kind="ExternalInput").ap()
    gb_i = nc.dram_tensor("gb", [128, DM], F32, kind="ExternalInput").ap()
    beb_i = nc.dram_tensor("beb", [128, DM], F32, kind="ExternalInput").ap()
    thb_i = nc.dram_tensor("thb", [128, DM], F32, kind="ExternalInput").ap()
    tbb_i = nc.dram_tensor("tbb", [128, 16], F32, kind="ExternalInput").ap()
    alpha2_i = nc.dram_tensor("alpha2", [128, 1], F32, kind="ExternalInput").ap()
    invar_i = nc.dram_tensor("invar", [128, 1], F32, kind="ExternalInput").ap()
    attn_o = nc.dram_tensor("attn", [SQ, DM], BF16, kind="ExternalOutput").ap()
    if DBG:
        dbg_o = nc.dram_tensor("dbg", [8, 128, DM], F32, kind="ExternalOutput").ap()

    W_state = ctx.sb("W_state", [128, 512], F32)
    W_mm = ctx.sb("W_mm", [128, 512], BF16)
    b_bc = ctx.sb("b_bc", [128, DM], F32)
    gb = ctx.sb("gbs", [128, DM], F32)
    beb = ctx.sb("bebs", [128, DM], F32)
    thb = ctx.sb("thbs", [128, DM], F32)
    tbb = ctx.sb("tbbs", [128, 16], F32)
    te64 = ctx.sb("te64s", [128, 1], F32)
    ones_bf = ctx.sb("ones_bf", [128, 128], BF16)
    qtl = [ctx.sb(f"qtl{i}", [128, DM], BF16) for i in range(2)]
    ktl = [ctx.sb(f"ktl{i}", [128, DM], BF16) for i in range(2)]
    vtl = [ctx.sb(f"vtl{i}", [128, DM], BF16) for i in range(2)]
    qTs = [ctx.sb(f"qTs{i}", [128, 512], BF16) for i in range(2)]
    kTs = [ctx.sb(f"kTs{i}", [128, 512], BF16) for i in range(2)]
    at_st = [ctx.sb(f"at_st{i}", [128, DM], BF16) for i in range(2)]
    xkf = ctx.sb("xkf", [128, DM], F32)
    xvf = ctx.sb("xvf", [128, DM], F32)
    xqf = ctx.sb("xqf", [128, DM], F32)
    wsA = ctx.sb("wsA", [128, DM], F32)
    wsB = ctx.sb("wsB", [128, DM], F32)
    wsC = ctx.sb("wsC", [128, DM], F32)
    wsD = ctx.sb("wsD", [128, DM], F32)
    wsE = ctx.sb("wsE", [128, DM], F32)
    wsF = ctx.sb("wsF", [128, DM], F32)
    wsG = ctx.sb("wsG", [128, DM], F32)
    exk_bf = ctx.sb("exk_bf", [128, DM], BF16)
    gradx_bf = ctx.sb("gradx_bf", [128, DM], BF16)
    egx_bf = ctx.sb("egx_bf", [128, DM], BF16)
    m16a = ctx.sb("m16a", [128, 16], F32)
    m16b = ctx.sb("m16b", [128, 16], F32)
    mu16 = ctx.sb("mu16", [128, 16], F32)
    rstd16 = ctx.sb("rstd16", [128, 16], F32)
    rs1 = ctx.sb("rs1", [128, 16], F32)
    rs2 = ctx.sb("rs2", [128, 16], F32)
    eta16 = ctx.sb("eta16", [128, 16], F32)
    psZ0 = ctx.ps("psZ0", [128, 512], F32)
    psZ1 = ctx.ps("psZ1", [128, 512], F32)
    psZq0 = ctx.ps("psZq0", [128, 512], F32)
    psZq1 = ctx.ps("psZq1", [128, 512], F32)
    psW = ctx.ps("psW", [128, 512], F32)
    psdb = ctx.ps("psdb", [128, DM], F32)

    vv, a, te, sy, gp = nc.vector, nc.scalar, nc.tensor, nc.sync, nc.gpsimd
    eps6 = ctx.constcol(1e-6)
    preload_set(nc, "natural_log_exp_and_others")

    for t, src in ((W_state, W1st), (b_bc, binit), (gb, gb_i), (beb, beb_i), (thb, thb_i), (tbb, tbb_i)):
        ctx.inc(sy.dma_start(out=t[:, :], in_=src[:, :]), "wld", 16)
    ctx.inc(sy.dma_start(out=te64[:, :], in_=alpha2_i[:, :]), "wld", 16)
    ctx.inc(sy.dma_start(out=m16a[:, 0:1], in_=invar_i[:, :]), "wld", 16)
    vv.memset(ones_bf[:, :], 1.0)
    ctx.inc(sy.dma_start(out=qtl[0][:, :], in_=q[0:128, :]), "qkv0", 16)
    ctx.inc(sy.dma_start(out=ktl[0][:, :], in_=k[0:128, :]), "qkv0", 16)
    ctx.inc(sy.dma_start(out=vtl[0][:, :], in_=v_in[0:128, :]), "qkv0", 16)
    ctx.inc(sy.dma_start(out=qTs[0].ap().rearrange("r (p k) -> r p k", p=8), in_=qT[0].rearrange("p r k -> r p k")), "qtl0", 16)
    ctx.inc(sy.dma_start(out=kTs[0].ap().rearrange("r (p k) -> r p k", p=8), in_=kT[0].rearrange("p r k -> r p k")), "qtl0", 16)
    ctx.wait(sy, "wld")
    ctx.bar()
    DBG_PRO = DBG
    ctx.ch(vv)(vv.tensor_add(te64[:, 0:1], te64[:, 0:1], m16a[:, 0:1]))
    ctx.ch(a)(a.activation(out=te64[:, 0:1], in_=te64[:, 0:1], func=AF.Relu, scale=1.0))
    ctx.ch(vv)(vv.tensor_scalar_mul(te64[:, 0:1], te64[:, 0:1], 1.0 / 64.0))
    ctx.ch(a)(a.activation(out=W_mm[:, :], in_=W_state[:, :], func=AF.Identity))
    ctx.bar()
    if DBG_PRO:
        ctx.inc(gp.dma_start(out=dbg_o[6][:, 0:512], in_=W_mm[:, :]), "dbg3", 16)
        ctx.inc(sy.dma_start(out=dbg_o[7][:, :], in_=b_bc[:, :]), "dbg4", 16)
        ctx.wait(gp, "dbg3")
        ctx.wait(sy, "dbg4")
        ctx.bar()

    st_at_tile = {}
    for c in range(NCH):
        it = c // 2
        rb = (c % 2) * 64
        sl = slice(rb, rb + 64)
        tb = it % 2
        cb = c % 2
        ktile, qtile, vtile = ktl[tb], qtl[tb], vtl[tb]
        kTsl, qTsl = kTs[cb], qTs[cb]
        att = at_st[tb]

        bidx = [0]

        def _dumpnow():
            ctx.bar()
            wide = (xkf, wsA, wsB, wsC, wsD, wsE, wsF, wsG)
            for di, dt_ in enumerate(wide):
                ctx.inc(sy.dma_start(out=dbg_o[di][0:64, :], in_=dt_[0:64, :]), "dbg2", 16)
            small = (mu16, rstd16, m16a, m16b, rs1, rs2, eta16, tbb)
            for dj, dt2 in enumerate(small):
                ctx.inc(sy.dma_start(out=dbg_o[0][64:128, dj * 16:(dj + 1) * 16], in_=dt2[64:128, 0:16]), "dbg2", 16) if False else None
                ctx.inc(sy.dma_start(out=dbg_o[1][64:128, dj * 16:(dj + 1) * 16], in_=dt2[0:64, 0:16].broadcast_to([64, 16]) if dt2.shape[1] == 1 else dt2[0:64, 0:16]), "dbg2", 16)
            ctx.inc(sy.dma_start(out=dbg_o[2][64:128, 0:1], in_=te64[0:64, 0:1], allow_slow_non_contiguous=True), "dbg2", 16)
            ctx.wait(sy, "dbg2")
            ctx.bar()

        def CH(eng):
            mark = ctx.ch(eng)

            def wrap(instr):
                n = mark(instr)
                bidx[0] += 1
                if DBG and c == 0 and bidx[0] == BCUT:
                    _dumpnow()
                return n
            return wrap

        def seg(ws):
            return ws.ap()[sl, :].rearrange("p (h d) -> p h d", h=16)

        def b16(m):
            return m.ap()[sl, :].unsqueeze(2).broadcast_to([64, 16, 64])

        if c + 1 < NCH:
            ctx.wait(sy, "mmE")
            ctx.inc(sy.dma_start(out=qTs[1 - cb].ap().rearrange("r (p k) -> r p k", p=8), in_=qT[c + 1].rearrange("p r k -> r p k")), f"qtl{1 - cb}", 16)
            ctx.inc(sy.dma_start(out=kTs[1 - cb].ap().rearrange("r (p k) -> r p k", p=8), in_=kT[c + 1].rearrange("p r k -> r p k")), f"qtl{1 - cb}", 16)
        if c % 2 == 0 and it + 1 < 16:
            t0n = (it + 1) * 128
            ctx.inc(sy.dma_start(out=qtl[1 - tb][:, :], in_=q[t0n:t0n + 128, :]), f"qkv{1 - tb}", 16)
            ctx.inc(sy.dma_start(out=ktl[1 - tb][:, :], in_=k[t0n:t0n + 128, :]), f"qkv{1 - tb}", 16)
            ctx.inc(sy.dma_start(out=vtl[1 - tb][:, :], in_=v_in[t0n:t0n + 128, :]), f"qkv{1 - tb}", 16)

        if STAGE < 1:
            continue
        ctx.wait(te, f"qtl{cb}", 32 * (c // 2 + 1))
        ctx.wait(te, f"qkv{tb}", 48 * (it // 2 + 1))
        ctx.wait(te, "p")
        for p8 in range(8):
            for j in range(2):
                psj = psZ0 if j == 0 else psZ1
                mm = te.matmul(psj[sl, p8 * 64:(p8 + 1) * 64],
                               kTsl[j * 64:(j + 1) * 64, p8 * 64:(p8 + 1) * 64],
                               W_mm[j * 64:(j + 1) * 64, p8 * 64:(p8 + 1) * 64],
                               start=True, stop=True, tile_position=(j * 64, rb))
        ctx.inc(mm, "mmA")

        if STAGE < 2:
            continue
        ctx.wait(a, f"qkv{tb}", 48 * (it // 2 + 1))
        CH(a)(a.activation(out=xkf[sl, :], in_=ktile[sl, :], func=AF.Identity))
        CH(a)(a.activation(out=xvf[sl, :], in_=vtile[sl, :], func=AF.Identity))
        CH(a)(a.activation(out=xqf[sl, :], in_=qtile[sl, :], func=AF.Identity))
        CH(vv)(vv.tensor_mul(wsC[sl, :], xkf[sl, :], thb[sl, :]))
        CH(vv)(vv.reduce_sum(m16a[sl, :], seg(wsC), axis=AX.X))
        CH(vv)(vv.tensor_add(m16a[sl, :], m16a[sl, :], tbb[sl, :]))
        CH(a)(a.activation(out=m16b[sl, :], in_=m16a[sl, :], func=AF.Exp, scale=-1.0))
        CH(vv)(vv.tensor_scalar_add(m16b[sl, :], m16b[sl, :], 1.0))
        CH(vv)(vv.reciprocal(m16b[sl, :], m16b[sl, :]))
        CH(vv)(vv.tensor_scalar(eta16[sl, :], m16b[sl, :], te64[sl, 0:1], None, op0=ALU.mult))
        ctx.wait(vv, "mmA")
        wsA4 = wsA.ap()[sl, :].rearrange("p (pp two d) -> p pp two d", two=2, d=64)
        bb4 = b_bc.ap()[sl, :].rearrange("p (pp two d) -> p pp two d", two=2, d=64)
        CH(vv)(vv.tensor_tensor(wsA4[:, :, 0, :], psZ0.ap()[sl, :].rearrange("p (pp d) -> p pp d", d=64), bb4[:, :, 0, :], op=ALU.add))
        nz1 = CH(vv)(vv.tensor_tensor(wsA4[:, :, 1, :], psZ1.ap()[sl, :].rearrange("p (pp d) -> p pp d", d=64), bb4[:, :, 1, :], op=ALU.add))
        CH(vv)(vv.reduce_sum(mu16[sl, :], seg(wsA), axis=AX.X))
        CH(vv)(vv.tensor_scalar_mul(mu16[sl, :], mu16[sl, :], 1.0 / 64.0))
        CH(vv)(vv.tensor_tensor(wsB.ap()[sl, :].rearrange("p (h d) -> p h d", h=16), seg(wsA), b16(mu16), op=ALU.subtract))
        CH(a)(a.activation(out=wsC[sl, :], in_=wsB[sl, :], func=AF.Square))
        CH(vv)(vv.reduce_sum(m16a[sl, :], seg(wsC), axis=AX.X))
        CH(a)(a.activation(out=m16a[sl, :], in_=m16a[sl, :], func=AF.Ln, scale=1.0 / 64.0, bias=eps6[sl, 0:1]))
        CH(a)(a.activation(out=rstd16[sl, :], in_=m16a[sl, :], func=AF.Exp, scale=-0.5))
        CH(vv)(vv.tensor_tensor(wsD.ap()[sl, :].rearrange("p (h d) -> p h d", h=16), seg(wsB), b16(rstd16), op=ALU.mult))
        CH(vv)(vv.tensor_mul(wsC[sl, :], wsD[sl, :], gb[sl, :]))
        CH(vv)(vv.tensor_add(wsC[sl, :], wsC[sl, :], beb[sl, :]))
        CH(vv)(vv.tensor_sub(wsE[sl, :], xvf[sl, :], xkf[sl, :]))
        CH(vv)(vv.tensor_sub(wsC[sl, :], wsC[sl, :], wsE[sl, :]))
        CH(vv)(vv.tensor_mul(wsE[sl, :], wsC[sl, :], gb[sl, :]))
        CH(vv)(vv.reduce_sum(rs1[sl, :], seg(wsE), axis=AX.X))
        CH(vv)(vv.tensor_scalar_mul(rs1[sl, :], rs1[sl, :], 1.0 / 64.0))
        CH(vv)(vv.tensor_mul(wsF[sl, :], wsE[sl, :], wsD[sl, :]))
        CH(vv)(vv.reduce_sum(rs2[sl, :], seg(wsF), axis=AX.X))
        CH(vv)(vv.tensor_scalar_mul(rs2[sl, :], rs2[sl, :], 1.0 / 64.0))
        CH(vv)(vv.tensor_tensor(wsF.ap()[sl, :].rearrange("p (h d) -> p h d", h=16), b16(rs2), seg(wsD), op=ALU.mult))
        CH(vv)(vv.tensor_tensor(wsC.ap()[sl, :].rearrange("p (h d) -> p h d", h=16), seg(wsE), b16(rs1), op=ALU.subtract))
        CH(vv)(vv.tensor_sub(wsC[sl, :], wsC[sl, :], wsF[sl, :]))
        CH(vv)(vv.tensor_tensor(wsG.ap()[sl, :].rearrange("p (h d) -> p h d", h=16), seg(wsC), b16(rstd16), op=ALU.mult))
        CH(a)(a.activation(out=gradx_bf[sl, :], in_=wsG[sl, :], func=AF.Identity))
        CH(vv)(vv.tensor_tensor(exk_bf.ap()[sl, :].rearrange("p (h d) -> p h d", h=16), seg(xkf), b16(eta16), op=ALU.mult))
        ngx = CH(vv)(vv.tensor_tensor(egx_bf.ap()[sl, :].rearrange("p (h d) -> p h d", h=16), seg(wsG), b16(eta16), op=ALU.mult))

        if STAGE < 3:
            continue
        ctx.wait(te, "p", ngx)
        for p8 in range(8):
            for j in range(2):
                h = 2 * p8 + j
                mm = te.matmul(psW[j * 64:(j + 1) * 64, p8 * 64:(p8 + 1) * 64],
                               exk_bf[sl, h * 64:(h + 1) * 64],
                               gradx_bf[sl, h * 64:(h + 1) * 64],
                               start=True, stop=True, tile_position=(rb, j * 64))
        ctx.inc(mm, "mmC")
        mm = te.matmul(psdb[:, 0:512], ones_bf[sl, :], egx_bf[sl, 0:512], start=True, stop=True,
                       tile_position=(rb, 0))
        mm = te.matmul(psdb[:, 512:1024], ones_bf[sl, :], egx_bf[sl, 512:1024], start=True, stop=True,
                       tile_position=(rb, 0))
        ctx.inc(mm, "mmC")

        if STAGE < 4:
            continue
        ctx.wait(vv, "mmC")
        ctx.ch(vv)(vv.tensor_sub(W_state[:, :], W_state[:, :], psW[:, :]))
        ctx.ch(vv)(vv.tensor_sub(b_bc[:, :], b_bc[:, :], psdb[:, :]))
        nst = ctx.ch(a)(a.activation(out=W_mm[:, :], in_=W_state[:, :], func=AF.Identity))

        if STAGE < 5:
            continue
        ctx.wait(te, "p", nst)
        for p8 in range(8):
            for j in range(2):
                psj = psZq0 if j == 0 else psZq1
                mm = te.matmul(psj[sl, p8 * 64:(p8 + 1) * 64],
                               qTsl[j * 64:(j + 1) * 64, p8 * 64:(p8 + 1) * 64],
                               W_mm[j * 64:(j + 1) * 64, p8 * 64:(p8 + 1) * 64],
                               start=True, stop=True, tile_position=(j * 64, rb))
        ctx.inc(mm, "mmE")

        if STAGE < 6:
            continue
        ctx.wait(vv, "mmE")
        wsA4 = wsA.ap()[sl, :].rearrange("p (pp two d) -> p pp two d", two=2, d=64)
        bb4 = b_bc.ap()[sl, :].rearrange("p (pp two d) -> p pp two d", two=2, d=64)
        ctx.ch(vv)(vv.tensor_tensor(wsA4[:, :, 0, :], psZq0.ap()[sl, :].rearrange("p (pp d) -> p pp d", d=64), bb4[:, :, 0, :], op=ALU.add))
        ctx.ch(vv)(vv.tensor_tensor(wsA4[:, :, 1, :], psZq1.ap()[sl, :].rearrange("p (pp d) -> p pp d", d=64), bb4[:, :, 1, :], op=ALU.add))
        ctx.ch(vv)(vv.reduce_sum(mu16[sl, :], seg(wsA), axis=AX.X))
        ctx.ch(vv)(vv.tensor_scalar_mul(mu16[sl, :], mu16[sl, :], 1.0 / 64.0))
        ctx.ch(vv)(vv.tensor_tensor(wsB.ap()[sl, :].rearrange("p (h d) -> p h d", h=16), seg(wsA), b16(mu16), op=ALU.subtract))
        ctx.ch(a)(a.activation(out=wsC[sl, :], in_=wsB[sl, :], func=AF.Square))
        ctx.ch(vv)(vv.reduce_sum(m16a[sl, :], seg(wsC), axis=AX.X))
        ctx.ch(a)(a.activation(out=m16a[sl, :], in_=m16a[sl, :], func=AF.Ln, scale=1.0 / 64.0, bias=eps6[sl, 0:1]))
        ctx.ch(a)(a.activation(out=rstd16[sl, :], in_=m16a[sl, :], func=AF.Exp, scale=-0.5))
        ctx.ch(vv)(vv.tensor_tensor(wsD.ap()[sl, :].rearrange("p (h d) -> p h d", h=16), seg(wsB), b16(rstd16), op=ALU.mult))
        ctx.ch(vv)(vv.tensor_mul(wsG[sl, :], wsD[sl, :], gb[sl, :]))
        ctx.ch(vv)(vv.tensor_add(wsG[sl, :], wsG[sl, :], beb[sl, :]))
        ctx.wait(vv, f"st{tb}", st_at_tile.get(it - 2, 0))
        nfin = ctx.ch(vv)(vv.tensor_add(att[sl, :], xqf[sl, :], wsG[sl, :]))

        if c % 2 == 1:
            ctx.wait(sy, "p", nfin)
            ctx.inc(sy.dma_start(out=attn_o[it * 128:(it + 1) * 128, :], in_=att[:, :]), f"st{tb}", 16)
            st_at_tile[it] = ctx.n(f"st{tb}")
    ctx.drain()
    ctx.close()
    return nc




def build_l0c():
    ctx = Ctx("l0c")
    nc = ctx.nc
    FF = 4096
    x = nc.dram_tensor("x", [SQ, DM], F32, kind="ExternalInput").ap()
    xnT_i = nc.dram_tensor("xnT", [DM, SQ], BF16, kind="ExternalInput").ap()
    s_i = nc.dram_tensor("s_ffn", [1, SQ], F32, kind="ExternalInput").ap()
    attn_i = nc.dram_tensor("attn", [SQ, DM], BF16, kind="ExternalInput").ap()
    wo = nc.dram_tensor("wo", [DM, DM], F32, kind="ExternalInput").ap()
    wupT = nc.dram_tensor("wupT", [DM, FF], F32, kind="ExternalInput").ap()
    wdnT = nc.dram_tensor("wdnT", [FF, DM], F32, kind="ExternalInput").ap()
    ident = nc.dram_tensor("ident", [128, 128], BF16, kind="ExternalInput").ap()
    x1_o = nc.dram_tensor("x1", [SQ, DM], F32, kind="ExternalOutput").ap()
    wdq_scr = nc.dram_tensor("wdq_scr", [FF, DM], BF16, kind="Internal").ap()
    rms_scr = nc.dram_tensor("rms_scr", [1, SQ], F32, kind="Internal").ap()

    v, a, te, sy, gp = nc.vector, nc.scalar, nc.tensor, nc.sync, nc.gpsimd
    wq_up = ctx.sb("wq_up", [128, 8 * FF], BF16)
    wf32 = ctx.sb("wf32", [128, FF], F32)
    wos = ctx.sb("wos", [128, 8 * DM], BF16)
    idt = ctx.sb("idt", [128, 128], BF16)
    xnTs = ctx.sb("xnTs", [128, 8 * 512], BF16)
    hbf = ctx.sb("hbf", [128, 32 * 512], BF16)
    hsqb = ctx.sb("hsqb", [128, 512], BF16)
    wdq = [ctx.sb(f"wdq{i}", [128, DM], BF16) for i in range(2)]
    srow = ctx.sb("srow", [1, SQ], BF16)
    sbc = ctx.sb("sbc", [128, 512], BF16)
    onesb = ctx.sb("onesb", [128, 128], BF16)
    shcol = ctx.sb("shcol", [128, NT], F32)
    attc = ctx.sb("attc", [128, DM], BF16)
    attT = ctx.sb("attT", [128, DM], BF16)
    xts = ctx.sb("xts", [128, DM], F32)
    ffns = ctx.sb("ffns", [128, DM], F32)
    gsil = ctx.sb("gsil", [128, DM], F32)
    x1s = ctx.sb("x1s", [128, DM], F32)
    red = ctx.sb("red", [128, 1], F32)
    acc = ctx.sb("acc", [128, 1], F32)
    alup = ctx.sb("alup", [128, 1], F32)
    aldn = ctx.sb("aldn", [128, 1], F32)
    bcol = ctx.sb("bcol", [128, 1], BF16)
    tmp = ctx.sb("tmpc", [128, FF], F32)
    sgn = ctx.sb("sgn", [128, FF], F32)

    psP = ctx.ps("psP", [128, 3584], F32)
    psT = ctx.ps("psTc", [128, 128], BF16)

    def bank(k):
        return psP[:, k * 512:(k + 1) * 512]

    v.memset(onesb[:, :], 1.0)
    ctx.inc(sy.dma_start(out=idt[:, :], in_=ident[:, :]), "wld", 16)
    ctx.inc(gp.dma_start(out=srow[0:1, :], in_=s_i[0:1, :]), "gld", 16)
    for kt in range(8):
        ctx.inc(gp.dma_start(out=wos[:, kt * DM:(kt + 1) * DM], in_=wo[kt * 128:(kt + 1) * 128, :]), "gld", 16)
    ctx.wait(sy, "wld")
    ctx.wait(gp, "gld")
    ctx.bar()

    alp_scr = nc.dram_tensor("alp_scr", [1, 130], F32, kind="Internal").ap()

    def alpha_pass(w_hbm, ntiles, width, out_col, nelem, tag):
        ctx.ch(v)(v.memset(acc[:, :], 0.0))
        for i in range(ntiles):
            ctx.wait(sy, "p", ctx.n("p"))
            ctx.inc(sy.dma_start(out=wf32[:, 0:width], in_=w_hbm[i * 128:(i + 1) * 128, :]), tag, 16)
            ctx.wait(v, tag, ctx.n(tag))
            ctx.ch(v)(v.reduce_sum(red[:, 0:1], wf32[:, 0:width], axis=AX.X, apply_absolute_value=True))
            nacc = ctx.ch(v)(v.tensor_add(acc[:, 0:1], acc[:, 0:1], red[:, 0:1]))
        ctx.wait(sy, "p", nacc)
        ctx.inc(sy.dma_start(out=alp_scr[0:1, 0:128].rearrange("o p -> p o"), in_=acc[:, 0:1]), f"{tag}b", 16)
        ctx.wait(sy, f"{tag}b", ctx.n(f"{tag}b"))
        ctx.inc(sy.dma_start(out=tmp[0:1, 0:128], in_=alp_scr[0:1, 0:128]), f"{tag}b", 16)
        ctx.wait(v, f"{tag}b", ctx.n(f"{tag}b"))
        ctx.ch(v)(v.reduce_sum(red[0:1, 0:1], tmp[0:1, 0:128], axis=AX.X))
        nthr = ctx.ch(v)(v.tensor_scalar_mul(red[0:1, 0:1], red[0:1, 0:1], 0.5 / nelem))
        ctx.wait(sy, "p", nthr)
        ctx.inc(sy.dma_start(out=alp_scr[0:1, 128:129], in_=red[0:1, 0:1]), f"{tag}b", 16)
        ctx.wait(sy, f"{tag}b", ctx.n(f"{tag}b"))
        ctx.inc(sy.dma_start(out=out_col[:, 0:1],
                             in_=alp_scr[0:1, 128:129].broadcast_to([128, 1])), f"{tag}b", 16)
        ctx.wait(v, f"{tag}b", ctx.n(f"{tag}b"))
        ctx.wait(a, f"{tag}b", ctx.n(f"{tag}b"))

    alpha_pass(wupT, 8, FF, alup, float(FF) * DM, "au")
    alpha_pass(wdnT, 32, DM, aldn, float(FF) * DM, "ad")
    ctx.bar()

    for i in range(8):
        ctx.wait(sy, "p", ctx.n("p"))
        ctx.inc(sy.dma_start(out=wf32[:, :], in_=wupT[i * 128:(i + 1) * 128, :]), "tu", 16)
        ctx.wait(a, "tu", ctx.n("tu"))
        ctx.ch(a)(a.activation(out=tmp[:, :], in_=wf32[:, :], func=AF.Abs))
        ctx.ch(a)(a.activation(out=sgn[:, :], in_=wf32[:, :], func=AF.Sign))
        ctx.ch(v)(v.tensor_scalar(tmp[:, :], tmp[:, :], alup[:, 0:1], None, op0=ALU.is_gt))
        ctx.ch(v)(v.tensor_tensor(wq_up[:, i * FF:(i + 1) * FF], tmp[:, :], sgn[:, :], op=ALU.mult))
    for i in range(32):
        ctx.wait(sy, "p", ctx.n("p"))
        ctx.wait(sy, f"tdst{i % 2}", 16 * (i // 2))
        ctx.inc(sy.dma_start(out=wf32[:, 0:DM], in_=wdnT[i * 128:(i + 1) * 128, :]), "td", 16)
        ctx.wait(a, "td", ctx.n("td"))
        ctx.ch(a)(a.activation(out=tmp[:, 0:DM], in_=wf32[:, 0:DM], func=AF.Abs))
        ctx.ch(a)(a.activation(out=sgn[:, 0:DM], in_=wf32[:, 0:DM], func=AF.Sign))
        ctx.ch(v)(v.tensor_scalar(tmp[:, 0:DM], tmp[:, 0:DM], aldn[:, 0:1], None, op0=ALU.is_gt))
        nq = ctx.ch(v)(v.tensor_tensor(wdq[i % 2][:, :], tmp[:, 0:DM], sgn[:, 0:DM], op=ALU.mult))
        ctx.wait(sy, "p", nq)
        ctx.inc(sy.dma_start(out=wdq_scr[i * 128:(i + 1) * 128, :], in_=wdq[i % 2][:, :]), f"tdst{i % 2}", 16)
    ctx.wait(sy, "tdst0")
    ctx.wait(sy, "tdst1")
    ctx.bar()

    for tc in range(4):
        c0 = tc * 512
        px = tc % 2
        for i in range(8):
            ctx.inc(sy.dma_start(out=xnTs[:, i * 512:(i + 1) * 512], in_=xnT_i[i * 128:(i + 1) * 128, c0:c0 + 512]), f"xn{px}", 16)
        ctx.ch(v)(v.tensor_copy(srow[0:1, c0:c0 + 512], srow[0:1, c0:c0 + 512])) if False else None
        ctx.wait(te, "p", ctx.n("p"))
        ctx.wait(te, "evD", ctx.n("evD"))
        ctx.inc(te.matmul(bank(3), onesb[0:1, 0:128], srow[0:1, c0:c0 + 512], start=True, stop=True), "mmS")
        ctx.wait(v, "mmS")
        ctx.ch(v)(v.tensor_copy(sbc[:, :], bank(3)))
        ctx.wait(v, f"xn{px}", ctx.n(f"xn{px}"))
        for i in range(8):
            ctx.ch(v)(v.tensor_mul(xnTs[:, i * 512:(i + 1) * 512], xnTs[:, i * 512:(i + 1) * 512], sbc[:, :]))
        nxn = ctx.n("p")
        ctx.wait(te, "p", nxn)
        for ft in range(32):
            ps = bank(ft % 2)
            ctx.wait(te, "ev", ctx.n("ev") - 1)
            for i in range(8):
                mm = te.matmul(ps, wq_up[:, i * FF + ft * 128: i * FF + (ft + 1) * 128],
                               xnTs[:, i * 512:(i + 1) * 512], start=(i == 0), stop=(i == 7))
            ctx.inc(mm, "mmU")
            ctx.wait(a, "mmU", ctx.n("mmU"))
            ctx.inc(a.activation(out=hbf[:, ft * 512:(ft + 1) * 512], in_=ps, func=AF.Gelu), "ev")
        ctx.wait(a, "ev", ctx.n("ev"))
        for ft in range(32):
            ctx.wait(a, "mmR", ctx.n("mmR"))
            ctx.ch(a)(a.activation(out=hsqb[:, :], in_=hbf[:, ft * 512:(ft + 1) * 512], func=AF.Square))
            nsq = ctx.n("p")
            ctx.wait(te, "p", nsq)
            ctx.inc(te.matmul(bank(2)[0:1, :], onesb[:, 0:1], hsqb[:, :], start=(ft == 0), stop=(ft == 31)), "mmR")
        ctx.wait(v, "mmR")
        nrr = ctx.ch(v)(v.tensor_copy(ffns[0:1, 0:512], bank(2)[0:1, :]))
        ctx.wait(sy, "p", nrr)
        ctx.inc(sy.dma_start(out=rms_scr[0:1, c0:c0 + 512], in_=ffns[0:1, 0:512]), "rmsst", 16)
        ctx.wait(sy, "rmsst")
        for j in range(4):
            ctx.inc(sy.dma_start(out=shcol[:, tc * 4 + j: tc * 4 + j + 1],
                                 in_=rms_scr[0:1, c0 + j * 128: c0 + (j + 1) * 128].rearrange("o p -> p o")), "shld", 16)
        ctx.wait(v, "shld")
        ctx.wait(a, "shld")
        sc = shcol[:, tc * 4:(tc + 1) * 4]
        ctx.ch(a)(a.activation(out=sc, in_=sc, func=AF.Sqrt, scale=1.0))
        ctx.ch(v)(v.tensor_scalar(sc, sc, 1.0 / 64.0, 1e-8, op0=ALU.mult, op1=ALU.add))
        nsh = ctx.ch(v)(v.reciprocal(sc, sc))
        for ts_ in range(4):
            tt = tc * 4 + ts_
            t0 = tt * 128
            ctx.wait(sy, "p", ctx._nx1_prev if hasattr(ctx, "_nx1_prev") else 0)
            ctx.inc(sy.dma_start(out=xts[:, :], in_=x[t0:t0 + 128, :]), "xld", 16)
            ctx.inc(sy.dma_start(out=attc[:, :], in_=attn_i[t0:t0 + 128, :]), "xld", 16)
            mmD_hist = getattr(ctx, "_mmD_hist", [])
            for ft in range(32):
                pw = ft % 2
                ctx.wait(sy, "mmD", mmD_hist[-2] if len(mmD_hist) >= 2 else 0)
                ctx.inc(sy.dma_start(out=wdq[pw][:, :], in_=wdq_scr[ft * 128:(ft + 1) * 128, :]), f"wd{pw}", 16)
                ctx.wait(te, f"wd{pw}", ctx.n(f"wd{pw}"))
                for oc in range(2):
                    mm = te.matmul(bank(4 + oc), hbf[:, ft * 512 + ts_ * 128: ft * 512 + (ts_ + 1) * 128],
                                   wdq[pw][:, oc * 512:(oc + 1) * 512], start=(ft == 0), stop=(ft == 31))
                    ctx.inc(mm, "mmD")
                mmD_hist.append(ctx.n("mmD"))
            ctx._mmD_hist = mmD_hist
            ndone = ctx.n("mmD")
            ctx.wait(v, "mmD", ndone)
            ctx.wait(v, "xld", ctx.n("xld"))
            ctx.ch(v)(v.tensor_scalar(ffns[:, 0:512], bank(4), shcol[:, tt:tt + 1], None, op0=ALU.mult))
            ctx.ch(v)(v.tensor_scalar(ffns[:, 512:1024], bank(5), shcol[:, tt:tt + 1], None, op0=ALU.mult))
            ctx.ch(a)(a.activation(out=gsil[:, :], in_=ffns[:, :], func=AF.Silu))
            for ib in range(8):
                ctx.wait(te, "p", ctx.n("p")) if ib == 0 else None
                ctx.wait(te, "tpe", ctx.n("tpe"))
                ctx.inc(te.transpose(psT[:, :], attc[:, ib * 128:(ib + 1) * 128], idt[:, :]), "tp")
                ctx.wait(a, "tp", ctx.n("tp"))
                ctx.inc(a.activation(out=attT[:, ib * 128:(ib + 1) * 128], in_=psT[:, :], func=AF.Identity), "tpe")
            ctx.wait(te, "tpe", ctx.n("tpe"))
            for oc in range(2):
                ps = bank(6) if oc == 0 else bank(3)
                for ib in range(8):
                    mm = te.matmul(ps, attT[:, ib * 128:(ib + 1) * 128],
                                   wos[:, ib * DM + oc * 512: ib * DM + (oc + 1) * 512],
                                   start=(ib == 0), stop=(ib == 7))
                ctx.inc(mm, "mmW")
            ctx.wait(v, "mmW", ctx.n("mmW"))
            ctx.wait(v, "cmb", max(0, ctx.n("cmb")))
            ctx.ch(v)(v.tensor_tensor(x1s[:, 0:512], bank(6), gsil[:, 0:512], op=ALU.mult))
            ctx.ch(v)(v.tensor_tensor(x1s[:, 512:1024], bank(3), gsil[:, 512:1024], op=ALU.mult))
            nx1 = ctx.ch(v)(v.tensor_add(x1s[:, :], x1s[:, :], xts[:, :]))
            ctx._nx1_prev = nx1
            ctx.wait(sy, "p", nx1)
            ctx.inc(sy.dma_start(out=x1_o[t0:t0 + 128, :], in_=x1s[:, :]), "cmb", 16)
            ctx.inc(v.tensor_copy(red[:, 0:1], red[:, 0:1]).then_inc(ctx.sem("evD"), 0) if False else v.tensor_copy(red[0:1, 0:1], red[0:1, 0:1]), "evD")
    ctx.drain()
    ctx.close()
    return nc




def emit_transposes(ctx, src_bf, n_blk, psT, dst_bf, idt, src_off=0, dst_off=0, blk_stride=128):
    nc = ctx.nc
    te, a = nc.tensor, nc.scalar
    for ib in range(n_blk):
        ctx.wait(te, "tpe", ctx.n("tpe"))
        ctx.inc(te.transpose(psT[:, :], src_bf[:, src_off + ib * 128: src_off + (ib + 1) * 128], idt[:, :]), "tp")
        ctx.wait(a, "tp", ctx.n("tp"))
        ctx.inc(a.activation(out=dst_bf[:, dst_off + ib * blk_stride: dst_off + ib * blk_stride + 128],
                             in_=psT[:, :], func=AF.Identity), "tpe")


def build_l1a():
    ctx = Ctx("l1a")
    nc = ctx.nc
    x = nc.dram_tensor("x", [SQ, DM], F32, kind="ExternalInput").ap()
    Wq = nc.dram_tensor("Wq", [DM, DM], F32, kind="ExternalInput").ap()
    bqb = nc.dram_tensor("bqb", [128, DM], F32, kind="ExternalInput").ap()
    ident = nc.dram_tensor("ident", [128, 128], BF16, kind="ExternalInput").ap()
    xn_o = nc.dram_tensor("xn", [SQ, DM], BF16, kind="ExternalOutput").ap()
    xnT_o = nc.dram_tensor("xnT", [DM, SQ], BF16, kind="ExternalOutput").ap()
    s1_o = nc.dram_tensor("s1", [SQ, DM], BF16, kind="ExternalOutput").ap()

    v, a, te, sy, gp = nc.vector, nc.scalar, nc.tensor, nc.sync, nc.gpsimd
    wqs = ctx.sb("wqs", [128, 8 * DM], BF16)
    bqs = ctx.sb("bqs", [128, DM], F32)
    idt = ctx.sb("idt", [128, 128], BF16)
    xts = [ctx.sb(f"xts{i}", [128, DM], F32) for i in range(2)]
    xn = ctx.sb("xn_s", [128, DM], BF16)
    xnT = ctx.sb("xnT_s", [128, DM], BF16)
    qtm = ctx.sb("qtm", [128, 16 * DM], BF16)
    qT = ctx.sb("qT", [128, 16 * DM], BF16)
    probs = ctx.sb("probs", [128, SQ], BF16)
    pT = ctx.sb("pT", [128, 128], BF16)
    srow = ctx.sb("srow_s", [128, DM], BF16)
    mx = ctx.sb("mx", [128, 4], F32)
    mxs = ctx.sb("mxs", [128, 1], F32)
    negms = ctx.sb("negms", [128, 1], F32)
    den = ctx.sb("den", [128, 4], F32)
    dsum = ctx.sb("dsum", [128, 1], F32)
    scr = {k: ctx.sb(k + "_1a", [128, 1], F32) for k in ("mu", "varsum", "rstd", "t1", "t2")}
    scr["cen"] = ctx.sb("cen_1a", [128, DM], F32)
    scr["sq"] = ctx.sb("sq_1a", [128, DM], F32)
    psP = ctx.ps("psP1a", [128, 3584], F32)
    psT = ctx.ps("psT1a", [128, 128], BF16)

    def bank(k):
        return psP[:, k * 512:(k + 1) * 512]

    ctx.constcol(1e-5)
    ctx.inc(sy.dma_start(out=idt[:, :], in_=ident[:, :]), "wld", 16)
    ctx.inc(sy.dma_start(out=bqs[:, :], in_=bqb[:, :]), "wld", 16)
    for kt in range(8):
        ctx.inc(gp.dma_start(out=wqs[:, kt * DM:(kt + 1) * DM], in_=Wq[kt * 128:(kt + 1) * 128, :]), "gld", 16)
    ctx.inc(sy.dma_start(out=xts[0][:, :], in_=x[0:128, :]), "xld0", 16)
    ctx.wait(sy, "wld")
    ctx.wait(gp, "gld")
    ctx.wait(sy, "xld0")
    ctx.bar()

    for it in range(NT):
        bi = it % 2
        t0 = it * 128
        if it + 1 < NT:
            ctx.inc(sy.dma_start(out=xts[1 - bi][:, :], in_=x[t0 + 128:t0 + 256, :]), f"xld{1 - bi}", 16)
        ctx.wait(v, f"xld{bi}", 16 * (it // 2 + 1))
        n = emit_ln(ctx, xts[bi], 128, DM, xn, scr, eps=1e-5)
        ctx.wait(sy, "p", n)
        ctx.inc(sy.dma_start(out=xn_o[t0:t0 + 128, :], in_=xn[:, :]), "st", 16)
        ctx.wait(te, "p", n)
        emit_transposes(ctx, xn, 8, psT, xnT, idt)
        ctx.wait(sy, "tpe", ctx.n("tpe"))
        for ib in range(8):
            ctx.inc(sy.dma_start(out=xnT_o[ib * 128:(ib + 1) * 128, t0:t0 + 128],
                                 in_=xnT[:, ib * 128:(ib + 1) * 128]), "st", 16)
        ctx.wait(te, "tpe", ctx.n("tpe"))
        for oc in range(2):
            ps = bank(oc)
            ctx.wait(te, "ev", ctx.n("ev") - 1)
            for kt in range(8):
                mm = te.matmul(ps, xnT[:, kt * 128:(kt + 1) * 128],
                               wqs[:, kt * DM + oc * 512: kt * DM + (oc + 1) * 512],
                               start=(kt == 0), stop=(kt == 7))
            ctx.inc(mm, "mm")
            ctx.wait(v, "mm", ctx.n("mm"))
            ctx.inc(v.tensor_tensor(qtm[:, it * DM + oc * 512: it * DM + (oc + 1) * 512],
                                    ps, bqs[:, oc * 512:(oc + 1) * 512], op=ALU.add), "ev")
        ctx.wait(te, "ev", ctx.n("ev"))
        for ib in range(8):
            ctx.wait(te, "tpe", ctx.n("tpe"))
            ctx.inc(te.transpose(psT[:, :], qtm[:, it * DM + ib * 128: it * DM + (ib + 1) * 128], idt[:, :]), "tp")
            ctx.wait(a, "tp", ctx.n("tp"))
            ctx.inc(a.activation(out=qT[:, ib * 2048 + t0: ib * 2048 + t0 + 128],
                                 in_=psT[:, :], func=AF.Identity), "tpe")
        ctx.wait(sy, "st")
        ctx.bar()

    SCALE = 1.0 / 32.0
    for it in range(NT):
        t0 = it * 128
        for kc in range(4):
            ps = bank(kc)
            for hb in range(8):
                mm = te.matmul(ps, qT[:, hb * 2048 + t0: hb * 2048 + t0 + 128],
                               qT[:, hb * 2048 + kc * 512: hb * 2048 + (kc + 1) * 512],
                               start=(hb == 0), stop=(hb == 7))
            ctx.inc(mm, "mmS")
        ctx.wait(v, "mmS", ctx.n("mmS"))
        for kc in range(4):
            ctx.ch(v)(v.tensor_reduce(mx[:, kc:kc + 1], bank(kc), axis=AX.X, op=ALU.max))
        ctx.ch(v)(v.tensor_reduce(mxs[:, 0:1], mx[:, 0:4], axis=AX.X, op=ALU.max))
        ctx.ch(v)(v.tensor_scalar_mul(negms[:, 0:1], mxs[:, 0:1], -SCALE))
        for kc in range(4):
            ctx.ch(a)(a.activation(out=probs[:, kc * 512:(kc + 1) * 512], in_=bank(kc),
                                   func=AF.Exp, scale=SCALE, bias=negms[:, 0:1]))
            ctx.ch(v)(v.reduce_sum(den[:, kc:kc + 1], probs[:, kc * 512:(kc + 1) * 512], axis=AX.X))
        ctx.ch(v)(v.reduce_sum(dsum[:, 0:1], den[:, 0:4], axis=AX.X))
        nsf = ctx.ch(v)(v.reciprocal(dsum[:, 0:1], dsum[:, 0:1]))
        for kt in range(16):
            if kt == 0:
                ctx.wait(te, "p", nsf)
            ctx.wait(te, "tpe", ctx.n("tpe"))
            ctx.inc(te.transpose(psT[:, :], probs[:, kt * 128:(kt + 1) * 128], idt[:, :]), "tp")
            ctx.wait(a, "tp", ctx.n("tp"))
            ctx.inc(a.activation(out=pT[:, :], in_=psT[:, :], func=AF.Identity), "tpe")
            ctx.wait(te, "tpe", ctx.n("tpe"))
            for oc in range(2):
                mm = te.matmul(bank(4 + oc), pT[:, :], qtm[:, kt * DM + oc * 512: kt * DM + (oc + 1) * 512],
                               start=(kt == 0), stop=(kt == 15))
            ctx.inc(mm, "mmA")
        ctx.wait(v, "mmA", ctx.n("mmA"))
        ctx.wait(v, "st1", ctx.n("st1"))
        ctx.ch(v)(v.tensor_scalar(srow[:, 0:512], bank(4), dsum[:, 0:1], None, op0=ALU.mult))
        nfin = ctx.ch(v)(v.tensor_scalar(srow[:, 512:1024], bank(5), dsum[:, 0:1], None, op0=ALU.mult))
        ctx.wait(sy, "p", nfin)
        ctx.inc(sy.dma_start(out=s1_o[t0:t0 + 128, :], in_=srow[:, :]), "st1", 16)
        ctx.wait(sy, "st1")
    ctx.drain()
    ctx.close()
    return nc




def build_l1b():
    ctx = Ctx("l1b")
    nc = ctx.nc
    TOK = B * SQ
    xnT_f = nc.dram_tensor("xnTf", [DM, TOK], BF16, kind="ExternalInput").ap()
    wihT_c = nc.dram_tensor("wihTc", [DM, 384], F32, kind="ExternalInput").ap()
    ctxwT = nc.dram_tensor("ctxwT", [DM, DM], F32, kind="ExternalInput").ap()
    ctxb = nc.dram_tensor("ctxb", [DM, 1], F32, kind="ExternalInput").ap()
    xw_o = nc.dram_tensor("xw", [TOK, 384], BF16, kind="ExternalOutput").ap()
    h0T_o = nc.dram_tensor("h0T", [DM, 8], F32, kind="ExternalOutput").ap()

    v, a, te, sy, gp = nc.vector, nc.scalar, nc.tensor, nc.sync, nc.gpsimd
    wih = ctx.sb("wih", [128, 8 * 384], BF16)
    cwT = ctx.sb("cwT", [128, 8 * DM], BF16)
    cbc = ctx.sb("cbc", [128, 8], F32)
    xnTs = [ctx.sb(f"xnTs{i}", [128, 8 * 1024], BF16) for i in range(2)]
    meanT = ctx.sb("meanT", [128, 64], F32)
    xws = [ctx.sb(f"xws{i}", [128, 384], BF16) for i in range(2)]
    h0s = ctx.sb("h0s", [128, 64], F32)
    psP = ctx.ps("psPb", [128, 3584], F32)

    def bank(k):
        return psP[:, k * 512:(k + 1) * 512]

    for kt in range(8):
        ctx.inc(gp.dma_start(out=wih[:, kt * 384:(kt + 1) * 384], in_=wihT_c[kt * 128:(kt + 1) * 128, :]), "gld", 16)
        ctx.inc(gp.dma_start(out=cwT[:, kt * DM:(kt + 1) * DM], in_=ctxwT[kt * 128:(kt + 1) * 128, :]), "gld", 16)
    ctx.inc(sy.dma_start(out=cbc[:, 0:1].ap() if False else cbc[:, 0:1], in_=ctxb[0:128, :]), "wld", 16)
    for g in range(1, 8):
        ctx.inc(sy.dma_start(out=cbc[:, g:g + 1], in_=ctxb[g * 128:(g + 1) * 128, :]), "wld", 16)
    v.memset(meanT[:, :], 0.0)
    for i in range(8):
        ctx.inc(sy.dma_start(out=xnTs[0][:, i * 1024:(i + 1) * 1024], in_=xnT_f[i * 128:(i + 1) * 128, 0:1024]), "xn0", 16)
    ctx.wait(sy, "wld")
    ctx.wait(gp, "gld")
    ctx.bar()

    for ch in range(16):
        bi = ch % 2
        c0 = ch * 1024
        if ch + 1 < 16:
            for i in range(8):
                ctx.inc(sy.dma_start(out=xnTs[1 - bi][:, i * 1024:(i + 1) * 1024],
                                     in_=xnT_f[i * 128:(i + 1) * 128, c0 + 1024:c0 + 2048]), f"xn{1 - bi}", 16)
        ctx.wait(te, f"xn{bi}", 128 * (ch // 2 + 1))
        ctx.wait(v, f"xn{bi}", 128 * (ch // 2 + 1))
        b_idx = c0 // 2048
        for i in range(8):
            ctx.ch(v)(v.reduce_sum(h0s[:, 0:1], xnTs[bi][:, i * 1024:(i + 1) * 1024], axis=AX.X))
            ctx.ch(v)(v.tensor_add(meanT[:, i * 8 + b_idx: i * 8 + b_idx + 1], meanT[:, i * 8 + b_idx: i * 8 + b_idx + 1], h0s[:, 0:1]))
        for ts_ in range(8):
            tt = ch * 8 + ts_
            pb = tt % 2
            ctx.wait(te, "ev", ctx.n("ev") - 1)
            for i in range(8):
                mm = te.matmul(bank(pb)[:, 0:384], xnTs[bi][:, i * 1024 + ts_ * 128: i * 1024 + (ts_ + 1) * 128],
                               wih[:, i * 384:(i + 1) * 384], start=(i == 0), stop=(i == 7))
            ctx.inc(mm, "mmX")
            ctx.wait(v, "mmX", ctx.n("mmX"))
            ctx.wait(v, f"xst{pb}", 16 * (tt // 2))
            ctx.inc(v.tensor_copy(xws[pb][:, :], bank(pb)[:, 0:384]), "ev")
            ctx.wait(sy, "ev", ctx.n("ev"))
            ctx.inc(sy.dma_start(out=xw_o[tt * 128:(tt + 1) * 128, :], in_=xws[pb][:, :]), f"xst{pb}", 16)
    nmt = ctx.ch(v)(v.tensor_copy(cwT[:, 0:0 + 64].ap() if False else h0s[:, :], meanT[:, :]))
    mtb = ctx.sb("mtb", [128, 64], BF16)
    nmt = ctx.ch(v)(v.tensor_copy(mtb[:, :], meanT[:, :]))
    ctx.wait(te, "p", nmt)
    for g in range(8):
        for i in range(8):
            mm = te.matmul(bank(2)[:, g * 8:(g + 1) * 8], cwT[:, i * DM + g * 128: i * DM + g * 128 + 128].ap().rearrange("p f -> p f") if False else cwT[:, i * DM + g * 128: i * DM + g * 128 + 128],
                           mtb[:, i * 8:(i + 1) * 8], start=(i == 0), stop=(i == 7))
    ctx.inc(mm, "mmH")
    ctx.wait(v, "mmH")
    for g in range(8):
        ctx.ch(v)(v.tensor_scalar(h0s[:, g * 8:(g + 1) * 8], bank(2)[:, g * 8:(g + 1) * 8],
                                  1.0 / 2048.0, cbc[:, g:g + 1], op0=ALU.mult, op1=ALU.add))
    nh0 = ctx.n("p")
    ctx.wait(sy, "p", nh0)
    for g in range(8):
        ctx.inc(sy.dma_start(out=h0T_o[g * 128:(g + 1) * 128, :], in_=h0s[:, g * 8:(g + 1) * 8]), "hst", 16)
    ctx.drain()
    ctx.close()
    return nc


def build_l1c(NITER=128, DBG=False):
    ctx = Ctx("l1c")
    nc = ctx.nc
    G3 = 3072
    xwp = nc.dram_tensor("xwp", [258, 8, 8 * G3], BF16, kind="ExternalInput").ap()
    hT0 = nc.dram_tensor("hT0", [128, 64], F32, kind="ExternalInput").ap()
    h0r = nc.dram_tensor("h0r", [8, DM], F32, kind="ExternalInput").ap()
    whhT = nc.dram_tensor("whhT", [DM, G3], F32, kind="ExternalInput").ap()
    i8b = nc.dram_tensor("i8b", [8, 8], BF16, kind="ExternalInput").ap()
    bmask = nc.dram_tensor("bmask", [8, 1], BF16, kind="ExternalInput").ap()
    hs_o = nc.dram_tensor("hs", [256, 8, DM], BF16, kind="ExternalOutput").ap()

    v, a, te, sy, gp = nc.vector, nc.scalar, nc.tensor, nc.sync, nc.gpsimd
    whh = ctx.sb("whh", [128, 8 * G3], BF16)
    xwt = [ctx.sb(f"xwt{i}", [8, 8 * G3], BF16) for i in range(2)]
    hT_bf = ctx.sb("hT_bf", [128, 64], BF16)
    hrow = ctx.sb("hrow", [8, DM], F32)
    hb_bf = ctx.sb("hb_bf", [8, DM], BF16)
    i8s = ctx.sb("i8s", [8, 8], BF16)
    bms = ctx.sb("bms", [8, 1], BF16)
    hstage = ctx.sb("hstage", [1, 8 * DM], BF16)
    r_g = ctx.sb("r_g", [8, DM], F32)
    z_g = ctx.sb("z_g", [8, DM], F32)
    n_g = ctx.sb("n_g", [8, DM], F32)
    t_g = ctx.sb("t_g", [8, DM], F32)
    psP = ctx.ps("psPc", [128, 3584], F32)

    def bank(k):
        return psP[:, k * 512:(k + 1) * 512]

    preload_set(nc, "sigmoid_and_others")
    for kt in range(8):
        ctx.inc(gp.dma_start(out=whh[:, kt * G3:(kt + 1) * G3], in_=whhT[kt * 128:(kt + 1) * 128, :]), "gld", 16)
    ctx.inc(gp.dma_start(out=hT_bf[:, :], in_=hT0[:, :]), "gld", 16)
    ctx.inc(sy.dma_start(out=hrow[0:8, :], in_=h0r[:, :]), "wld", 16)
    ctx.inc(sy.dma_start(out=i8s[:, :], in_=i8b[:, :]), "wld", 16)
    ctx.inc(sy.dma_start(out=bms[:, :], in_=bmask[:, :]), "wld", 16)
    ctx.inc(sy.dma_start(out=xwt[0][0:8, :], in_=xwp[0, :, :]), "xw0", 16)
    ctx.wait(sy, "wld")
    ctx.wait(sy, "xw0")
    ctx.wait(gp, "gld")
    ctx.bar()

    if DBG:
        dbg_o = nc.dram_tensor("dbg", [8, 8, DM], F32, kind="ExternalOutput").ap()
    loop_sems = ["mmG", "mmT", "pp", "xwa", "xwb", "hst"]
    for nm in loop_sems:
        ctx.sem(nm)
    semxw = {0: ctx.sem("xwa"), 1: ctx.sem("xwb")}

    def emit_sub(par, st):
        for s_ in range(8):
            xw_t = xwt[par][0:8, s_ * G3:(s_ + 1) * G3]
            if st["pp"] > 0:
                nc.tensor.wait_ge(ctx.sem("pp"), st["pp"])
            for oc in range(6):
                for kt in range(8):
                    mm = nc.tensor.matmul(bank(oc)[0:8, :], hT_bf[:, kt * 8:(kt + 1) * 8],
                                          whh[:, kt * G3 + oc * 512: kt * G3 + (oc + 1) * 512],
                                          start=(kt == 0), stop=(kt == 7))
            st["mmG"] += 1
            mm.then_inc(ctx.sem("mmG"), 1)
            nc.vector.wait_ge(ctx.sem("mmG"), st["mmG"])

            def TT(out, a_, b_, op):
                nc.vector.wait_ge(ctx.sem("pp"), st["pp"])
                ins = nc.vector.tensor_tensor(out, a_, b_, op=op)
                st["pp"] += 1
                ins.then_inc(ctx.sem("pp"), 1)

            def ACT(out, in_, func):
                nc.scalar.wait_ge(ctx.sem("pp"), st["pp"])
                ins = nc.scalar.activation(out=out, in_=in_, func=func)
                st["pp"] += 1
                ins.then_inc(ctx.sem("pp"), 1)

            TT(t_g[0:8, 0:512], bank(0)[0:8, :], xw_t[:, 0:512], ALU.add)
            TT(t_g[0:8, 512:1024], bank(1)[0:8, :], xw_t[:, 512:1024], ALU.add)
            ACT(r_g[0:8, :], t_g[0:8, :], AF.Sigmoid)
            TT(t_g[0:8, 0:512], bank(2)[0:8, :], xw_t[:, 1024:1536], ALU.add)
            TT(t_g[0:8, 512:1024], bank(3)[0:8, :], xw_t[:, 1536:2048], ALU.add)
            ACT(z_g[0:8, :], t_g[0:8, :], AF.Sigmoid)
            TT(t_g[0:8, 0:512], r_g[0:8, 0:512], bank(4)[0:8, :], ALU.mult)
            TT(t_g[0:8, 512:1024], r_g[0:8, 512:1024], bank(5)[0:8, :], ALU.mult)
            TT(t_g[0:8, :], t_g[0:8, :], xw_t[:, 2048:3072], ALU.add)
            ACT(n_g[0:8, :], t_g[0:8, :], AF.Tanh)
            TT(t_g[0:8, :], hrow[0:8, :], n_g[0:8, :], ALU.subtract)
            TT(t_g[0:8, :], z_g[0:8, :], t_g[0:8, :], ALU.mult)
            TT(hrow[0:8, :], n_g[0:8, :], t_g[0:8, :], ALU.add)
            ACT(hb_bf[0:8, :], hrow[0:8, :], AF.Identity)
            nc.tensor.wait_ge(ctx.sem("pp"), st["pp"])
            for g in range(8):
                nc.tensor.matmul(bank(6)[:, g * 8:(g + 1) * 8], hb_bf[0:8, g * 128:(g + 1) * 128],
                                 i8s[0:8, :], start=True, stop=True)
            for half in range(2):
                mm = nc.tensor.matmul(bank(half)[0:1, :], bms[0:8, 0:1],
                                      hb_bf[0:8, half * 512:(half + 1) * 512], start=True, stop=True)
            st["mmT"] += 1
            mm.then_inc(ctx.sem("mmT"), 1)
            nc.vector.wait_ge(ctx.sem("mmT"), st["mmT"])
            nc.vector.wait_ge(ctx.sem("pp"), st["pp"])
            ins = nc.vector.tensor_copy(hT_bf[:, :], bank(6)[:, 0:64])
            st["pp"] += 1
            ins.then_inc(ctx.sem("pp"), 1)
            for half in range(2):
                nc.vector.wait_ge(ctx.sem("pp"), st["pp"])
                if half == 0:
                    nc.vector.wait_ge(ctx.sem("hst"), st["hst"])
                ins = nc.vector.tensor_copy(hstage[0:1, s_ * DM + half * 512: s_ * DM + (half + 1) * 512],
                                            bank(half)[0:1, :])
                st["pp"] += 1
                ins.then_inc(ctx.sem("pp"), 1)

    with nc.Fori(0, NITER) as it_:
        i2 = nc.sync.snap(it_)
        st = {"mmG": 0, "mmT": 0, "pp": 0, "hst": 0}
        nc.sync.dma_start(out=xwt[1][0:8, :], in_=xwp[bass.ds(2 * i2 + 1, 1), :, :]).then_inc(semxw[1], 16)
        emit_sub(0, st)
        nc.sync.wait_ge(ctx.sem("pp"), st["pp"])
        st["hst"] += 16
        nc.sync.dma_start(out=hs_o[bass.ds(2 * i2, 1), :, :].rearrange("o b f -> o (b f)"),
                          in_=hstage[0:1, :]).then_inc(ctx.sem("hst"), 16)
        nc.sync.dma_start(out=xwt[0][0:8, :], in_=xwp[bass.ds(2 * i2 + 2, 1), :, :]).then_inc(semxw[0], 16)
        nc.sync.wait_ge(semxw[1], 16)
        nc.vector.wait_ge(semxw[1], 16)
        nc.tensor.wait_ge(semxw[1], 16)
        emit_sub(1, st)
        nc.sync.wait_ge(ctx.sem("pp"), st["pp"])
        st["hst"] += 16
        nc.sync.dma_start(out=hs_o[bass.ds(2 * i2 + 1, 1), :, :].rearrange("o b f -> o (b f)"),
                          in_=hstage[0:1, :]).then_inc(ctx.sem("hst"), 16)
        nc.sync.wait_ge(ctx.sem("hst"), st["hst"])
        nc.sync.wait_ge(semxw[0], 16)
        nc.all_engine_barrier()
        for nm in loop_sems:
            nc.gpsimd.sem_clear(ctx.sem(nm))
        nc.all_engine_barrier()
    ctx.bar()
    if DBG:
        for di, dt_ in enumerate((r_g, z_g, n_g, t_g, hrow, hb_bf)):
            ctx.inc(gp.dma_start(out=dbg_o[di][0:8, :], in_=dt_[0:8, :]), "dbgl", 16)
        ctx.wait(gp, "dbgl")
        ctx.bar()
    ctx.drain()
    ctx.close()
    return nc




def build_l1d():
    ctx = Ctx("l1d")
    nc = ctx.nc
    hs_i = nc.dram_tensor("hs", [SQ, DM], BF16, kind="ExternalInput").ap()
    xnT_i = nc.dram_tensor("xnT", [DM, SQ], BF16, kind="ExternalInput").ap()
    exprT = nc.dram_tensor("exprT", [DM, DM], F32, kind="ExternalInput").ap()
    eyef = nc.dram_tensor("eyef", [128, 64], F32, kind="ExternalInput").ap()
    cos_o = nc.dram_tensor("cos", [SQ, 8], F32, kind="ExternalOutput").ap()
    pen_o = nc.dram_tensor("pen", [1, 1], F32, kind="ExternalOutput").ap()
    bscr = nc.dram_tensor("bscr_d", [1, 128], F32, kind="Internal").ap()

    v, a, te, sy, gp = nc.vector, nc.scalar, nc.tensor, nc.sync, nc.gpsimd
    ew = ctx.sb("ew", [128, 8 * DM], BF16)
    eye = ctx.sb("eye", [128, 64], F32)
    hst = [ctx.sb(f"hst{i}", [128, DM], BF16) for i in range(2)]
    xnTs = [ctx.sb(f"xnTs{i}", [128, DM], BF16) for i in range(2)]
    hsn = ctx.sb("hsn", [128, DM], BF16)
    sqf = ctx.sb("sqf", [128, DM], F32)
    exprf = ctx.sb("exprf", [128, DM], F32)
    gram = ctx.sb("gram", [128, 64], F32)
    m8a = ctx.sb("m8a", [128, 8], F32)
    m8b = ctx.sb("m8b", [128, 8], F32)
    m8c = ctx.sb("m8c", [128, 8], F32)
    nrm = ctx.sb("nrm", [128, 8], F32)
    rinv = ctx.sb("rinv", [128, 8], F32)
    cosb = ctx.sb("cosb", [128, 8], F32)
    acc = ctx.sb("accd", [128, 1], F32)
    red = ctx.sb("redd", [128, 1], F32)
    scrt = ctx.sb("scrt", [128, 128], F32)
    psP = ctx.ps("psPd", [128, 3584], F32)

    def bank(k):
        return psP[:, k * 512:(k + 1) * 512]

    preload_set(nc, "sqrt_and_others")
    for kt in range(8):
        ctx.inc(gp.dma_start(out=ew[:, kt * DM:(kt + 1) * DM], in_=exprT[kt * 128:(kt + 1) * 128, :]), "gld", 16)
    ctx.inc(sy.dma_start(out=eye[:, :], in_=eyef[:, :]), "wld", 16)
    ctx.inc(sy.dma_start(out=hst[0][:, :], in_=hs_i[0:128, :]), "hl0", 16)
    for ib in range(8):
        ctx.inc(sy.dma_start(out=xnTs[0][:, ib * 128:(ib + 1) * 128], in_=xnT_i[ib * 128:(ib + 1) * 128, 0:128]), "hl0", 16)
    v.memset(acc[:, :], 0.0)
    ctx.wait(sy, "wld")
    ctx.wait(sy, "hl0")
    ctx.wait(gp, "gld")
    ctx.bar()

    for it in range(NT):
        bi = it % 2
        t0 = it * 128
        if it + 1 < NT:
            ctx.inc(sy.dma_start(out=hst[1 - bi][:, :], in_=hs_i[t0 + 128:t0 + 256, :]), f"hl{1 - bi}", 16)
            for ib in range(8):
                ctx.inc(sy.dma_start(out=xnTs[1 - bi][:, ib * 128:(ib + 1) * 128],
                                     in_=xnT_i[ib * 128:(ib + 1) * 128, t0 + 128:t0 + 256]), f"hl{1 - bi}", 16)
        ctx.wait(a, f"hl{bi}", 144 * (it // 2 + 1))
        ctx.wait(v, f"hl{bi}", 144 * (it // 2 + 1))
        ht = hst[bi]
        ctx.ch(a)(a.activation(out=sqf[:, :], in_=ht[:, :], func=AF.Square))
        ctx.ch(v)(v.reduce_sum(m8a[:, :], sqf.ap().rearrange("p (e d) -> p e d", e=8), axis=AX.X))
        ctx.ch(a)(a.activation(out=nrm[:, :], in_=m8a[:, :], func=AF.Sqrt))
        ctx.ch(v)(v.tensor_scalar_max(m8b[:, :], nrm[:, :], 1e-12))
        ctx.ch(v)(v.reciprocal(rinv[:, :], m8b[:, :]))
        ctx.ch(v)(v.tensor_tensor(hsn.ap().rearrange("p (e d) -> p e d", e=8),
                                  ht.ap().rearrange("p (e d) -> p e d", e=8),
                                  rinv.ap().unsqueeze(2).broadcast_to([128, 8, 128]), op=ALU.mult))
        hsn3 = hsn.ap().rearrange("p (f d) -> p f d", f=8)
        sqf3 = sqf.ap().rearrange("p (f d) -> p f d", f=8)
        for e in range(8):
            esl = hsn.ap()[:, e * 128:(e + 1) * 128].unsqueeze(1).broadcast_to([128, 8, 128])
            ctx.ch(v)(v.tensor_tensor(sqf3, hsn3, esl, op=ALU.mult))
            ctx.ch(v)(v.reduce_sum(gram[:, e * 8:(e + 1) * 8], sqf3, axis=AX.X))
        ctx.ch(v)(v.tensor_sub(gram[:, :], gram[:, :], eye[:, :]))
        ctx.ch(a)(a.activation(out=gram[:, :], in_=gram[:, :], func=AF.Square))
        ctx.ch(v)(v.reduce_sum(m8a[:, :], gram.ap().rearrange("p (e f) -> p e f", e=8), axis=AX.X))
        ctx.ch(a)(a.activation(out=m8b[:, :], in_=m8a[:, :], func=AF.Sqrt))
        ctx.ch(v)(v.tensor_scalar_max(m8c[:, :], m8b[:, :], 1e-12))
        ctx.ch(v)(v.reciprocal(m8c[:, :], m8c[:, :]))
        ctx.ch(v)(v.tensor_mul(m8b[:, :], m8b[:, :], m8c[:, :]))
        ctx.ch(v)(v.tensor_mul(m8b[:, :], m8b[:, :], m8b[:, :]))
        ctx.ch(v)(v.reduce_sum(red[:, 0:1], m8b[:, :], axis=AX.X))
        ctx.ch(v)(v.tensor_add(acc[:, 0:1], acc[:, 0:1], red[:, 0:1]))
        ctx.wait(te, "p", ctx.n("p"))
        for oc in range(2):
            for kt in range(8):
                mm = te.matmul(bank(oc), xnTs[bi][:, kt * 128:(kt + 1) * 128],
                               ew[:, kt * DM + oc * 512: kt * DM + (oc + 1) * 512],
                               start=(kt == 0), stop=(kt == 7))
            ctx.inc(mm, "mmE")
        ctx.wait(v, "mmE", ctx.n("mmE"))
        ctx.ch(v)(v.tensor_copy(exprf[:, 0:512], bank(0)))
        ctx.ch(v)(v.tensor_copy(exprf[:, 512:1024], bank(1)))
        ctx.ch(v)(v.tensor_tensor(sqf.ap().rearrange("p (e d) -> p e d", e=8),
                                  exprf.ap().rearrange("p (e d) -> p e d", e=8),
                                  hsn.ap().rearrange("p (e d) -> p e d", e=8), op=ALU.mult))
        ctx.ch(v)(v.reduce_sum(m8a[:, :], sqf.ap().rearrange("p (e d) -> p e d", e=8), axis=AX.X))
        ctx.ch(a)(a.activation(out=sqf[:, :], in_=exprf[:, :], func=AF.Square))
        ctx.ch(v)(v.reduce_sum(m8b[:, :], sqf.ap().rearrange("p (e d) -> p e d", e=8), axis=AX.X))
        ctx.ch(a)(a.activation(out=m8b[:, :], in_=m8b[:, :], func=AF.Sqrt))
        ctx.ch(v)(v.tensor_scalar_max(m8b[:, :], m8b[:, :], 1e-8))
        ctx.ch(v)(v.tensor_mul(m8c[:, :], nrm[:, :], rinv[:, :]))
        ctx.ch(v)(v.tensor_scalar_max(m8c[:, :], m8c[:, :], 1e-8))
        ctx.ch(v)(v.tensor_mul(m8b[:, :], m8b[:, :], m8c[:, :]))
        ctx.ch(v)(v.reciprocal(m8b[:, :], m8b[:, :]))
        ctx.ch(v)(v.tensor_mul(m8a[:, :], m8a[:, :], m8b[:, :]))
        ctx.wait(v, "cst", max(0, ctx.n("cst")))
        ctx.ch(v)(v.tensor_scalar(cosb[:, :], m8a[:, :], -1.0, 1.0, op0=ALU.mult, op1=ALU.add))
        nfin = ctx.n("p")
        ctx.wait(sy, "p", nfin)
        ctx.wait(sy, "cst", max(0, ctx.n("cst") - 16))
        ctx.inc(sy.dma_start(out=cos_o[t0:t0 + 128, :], in_=cosb[:, :]), "cst", 16)
    nacc = ctx.n("p")
    ctx.wait(sy, "p", nacc)
    ctx.wait(sy, "cst")
    ctx.inc(sy.dma_start(out=bscr[0:1, 0:128].rearrange("o p -> p o"), in_=acc[:, 0:1]), "bb", 16)
    ctx.wait(sy, "bb")
    ctx.inc(sy.dma_start(out=scrt[0:1, 0:128], in_=bscr[0:1, :]), "bb", 16)
    ctx.wait(v, "bb", ctx.n("bb"))
    ctx.ch(v)(v.reduce_sum(red[0:1, 0:1], scrt[0:1, 0:128], axis=AX.X))
    npen = ctx.n("p")
    ctx.wait(sy, "p", npen)
    ctx.inc(sy.dma_start(out=pen_o[0:1, 0:1], in_=red[0:1, 0:1]), "pst", 16)
    ctx.drain()
    ctx.close()
    return nc


def build_l1e():
    ctx = Ctx("l1e")
    nc = ctx.nc
    FE = 2048
    x1 = nc.dram_tensor("x1", [SQ, DM], F32, kind="ExternalInput").ap()
    s1 = nc.dram_tensor("s1", [SQ, DM], BF16, kind="ExternalInput").ap()
    xnT_i = nc.dram_tensor("xnT", [DM, SQ], BF16, kind="ExternalInput").ap()
    cos_i = nc.dram_tensor("cos", [SQ, 8], F32, kind="ExternalInput").ap()
    pen8 = nc.dram_tensor("pen8", [1, 8], F32, kind="ExternalInput").ap()
    gwT = nc.dram_tensor("gwT", [9, DM, FE], F32, kind="ExternalInput").ap()
    uwT = nc.dram_tensor("uwT", [9, DM, FE], F32, kind="ExternalInput").ap()
    dwT = nc.dram_tensor("dwT", [9, FE, DM], F32, kind="ExternalInput").ap()
    x2_o = nc.dram_tensor("x2", [SQ, DM], F32, kind="ExternalOutput").ap()
    gscr = nc.dram_tensor("gscr", [9 * DM, FE], BF16, kind="Internal").ap()
    uscr = nc.dram_tensor("uscr", [9 * DM, FE], BF16, kind="Internal").ap()
    dscr = nc.dram_tensor("dscr", [9 * FE, DM], BF16, kind="Internal").ap()
    bscr = nc.dram_tensor("bscr_e", [1, 130], F32, kind="Internal").ap()

    v, a, te, sy, gp = nc.vector, nc.scalar, nc.tensor, nc.sync, nc.gpsimd
    wload = [ctx.sb(f"wload{i}", [128, FE], BF16) for i in range(2)]
    gws = ctx.sb("gws", [128, 8 * FE], BF16)
    uws = ctx.sb("uws", [128, 8 * FE], BF16)
    dws = ctx.sb("dws", [128, 16 * DM], BF16)
    xnTs = ctx.sb("xnTse", [128, 8 * 512], BF16)
    pr = ctx.sb("pr", [128, 16 * 512], BF16)
    moe = ctx.sb("moe", [128, 8 * DM], F32)
    cwt = ctx.sb("cwt", [128, 16 * 9], F32)
    m8a = ctx.sb("m8ae", [128, 8], F32)
    m8b = ctx.sb("m8be", [128, 8], F32)
    m1 = ctx.sb("m1e", [128, 1], F32)
    m2 = ctx.sb("m2e", [128, 1], F32)
    pd = ctx.sb("pde", [128, 1], F32)
    p1 = ctx.sb("p1e", [128, 1], F32)
    pfac = ctx.sb("pfac", [128, 1], F32)
    penr = ctx.sb("penr", [1, 8], F32)
    xts = ctx.sb("xtse", [128, DM], F32)
    sts = ctx.sb("stse", [128, DM], BF16)
    gsil = ctx.sb("gsile", [128, DM], F32)
    x2s = ctx.sb("x2se", [128, DM], F32)
    cost = ctx.sb("cost", [128, 8], F32)
    sil_f = [ctx.sb("sil_f0", [128, 512], F32), ctx.sb("sil_f1", [128, 512], F32)]
    psP = ctx.ps("psPe", [128, 3584], F32)

    def bank(k):
        return psP[:, k * 512:(k + 1) * 512]

    ctx.inc(sy.dma_start(out=penr[0:1, :], in_=pen8[0:1, :]), "wld", 16)
    ctx.wait(sy, "wld")
    ctx.wait(v, "wld", ctx.n("wld"))
    ctx.ch(v)(v.reduce_sum(m1[0:1, 0:1], penr[0:1, 0:8], axis=AX.X))
    nps = ctx.ch(v)(v.tensor_scalar(m2[0:1, 0:1], m1[0:1, 0:1], 1.0 / 16384.0, 1.0, op0=ALU.mult, op1=ALU.add))
    ctx.wait(sy, "p", nps)
    ctx.inc(sy.dma_start(out=bscr[0:1, 128:129], in_=m2[0:1, 0:1]), "bb", 16)
    ctx.wait(sy, "bb")
    ctx.inc(sy.dma_start(out=pfac[:, 0:1], in_=bscr[0:1, 128:129].broadcast_to([128, 1])), "bb", 16)
    cvi = 0
    for wt, scr_t, ntile, wid in ((gwT, gscr, 8, FE), (uwT, uscr, 8, FE), (dwT, dscr, 16, DM)):
        for e in range(9):
            for kt in range(ntile):
                pb = cvi % 2
                ctx.wait(gp, f"wcs{pb}", 16 * (cvi // 2))
                ctx.inc(gp.dma_start(out=wload[pb][:, 0:wid], in_=wt[e, kt * 128:(kt + 1) * 128, :]), f"wcv{pb}", 16)
                ctx.wait(sy, f"wcv{pb}", 16 * (cvi // 2 + 1))
                ctx.inc(sy.dma_start(out=scr_t[e * (ntile * 128) + kt * 128: e * (ntile * 128) + (kt + 1) * 128, :],
                                     in_=wload[pb][:, 0:wid]), f"wcs{pb}", 16)
                cvi += 1
    ctx.wait(sy, "wcs0")
    ctx.wait(sy, "wcs1")
    ctx.wait(gp, "wcv0")
    ctx.wait(gp, "wcv1")
    ctx.bar()

    preload_set(nc, "exp_and_others")
    for it in range(NT):
        t0 = it * 128
        ctx.wait(sy, "p", ctx.n("p"))
        ctx.inc(sy.dma_start(out=cost[:, :], in_=cos_i[t0:t0 + 128, :]), "cld", 16)
        ctx.wait(v, "cld", ctx.n("cld"))
        ctx.wait(v, "bb", ctx.n("bb"))
        ctx.ch(v)(v.tensor_scalar(m8a[:, :], cost[:, :], pfac[:, 0:1], None, op0=ALU.mult))
        ctx.ch(v)(v.tensor_reduce(m1[:, 0:1], m8a[:, :], axis=AX.X, op=ALU.max))
        ctx.ch(v)(v.tensor_scalar(m8b[:, :], m8a[:, :], m1[:, 0:1], None, op0=ALU.is_equal))
        ctx.ch(v)(v.scalar_tensor_tensor(m8a[:, :], m8b[:, :], -1e30, m8a[:, :], op0=ALU.mult, op1=ALU.add))
        ctx.ch(v)(v.tensor_reduce(m2[:, 0:1], m8a[:, :], axis=AX.X, op=ALU.max))
        ctx.ch(v)(v.tensor_sub(pd[:, 0:1], m2[:, 0:1], m1[:, 0:1]))
        ctx.ch(a)(a.activation(out=pd[:, 0:1], in_=pd[:, 0:1], func=AF.Exp))
        ctx.ch(v)(v.tensor_scalar_add(p1[:, 0:1], pd[:, 0:1], 1.0))
        ctx.ch(v)(v.reciprocal(p1[:, 0:1], p1[:, 0:1]))
        ctx.ch(v)(v.tensor_mul(pd[:, 0:1], pd[:, 0:1], p1[:, 0:1]))
        ctx.ch(v)(v.tensor_scalar(m8a[:, :], m8a[:, :], m2[:, 0:1], None, op0=ALU.is_equal))
        ctx.ch(v)(v.tensor_scalar(m8b[:, :], m8b[:, :], p1[:, 0:1], None, op0=ALU.mult))
        ctx.ch(v)(v.tensor_scalar(m8a[:, :], m8a[:, :], pd[:, 0:1], None, op0=ALU.mult))
        ctx.ch(v)(v.tensor_add(cwt[:, it * 9: it * 9 + 8], m8a[:, :], m8b[:, :]))
        ctx.ch(v)(v.memset(cwt[:, it * 9 + 8: it * 9 + 9], 1.0))
    ctx.bar()

    preload_set(nc, "silu_and_others")
    for th in range(2):
        h0_ = th * 1024
        ctx.ch(v)(v.memset(moe[:, :], 0.0))
        ctx.bar()
        for e in range(9):
            ctx.wait(sy, "mmG", ctx.n("mmG"))
            ctx.wait(sy, "mmD", ctx.n("mmD"))
            for kt in range(8):
                ctx.inc(sy.dma_start(out=gws[:, kt * FE:(kt + 1) * FE],
                                     in_=gscr[e * DM + kt * 128: e * DM + (kt + 1) * 128, :]), "ewl", 16)
                ctx.inc(sy.dma_start(out=uws[:, kt * FE:(kt + 1) * FE],
                                     in_=uscr[e * DM + kt * 128: e * DM + (kt + 1) * 128, :]), "ewl", 16)
            for kt in range(16):
                ctx.inc(sy.dma_start(out=dws[:, kt * DM:(kt + 1) * DM],
                                     in_=dscr[e * FE + kt * 128: e * FE + (kt + 1) * 128, :]), "ewl", 16)
            ctx.wait(te, "ewl", ctx.n("ewl"))
            for tc in range(2):
                prodp_hist = [0]
                c0 = h0_ + tc * 512
                ctx.wait(sy, "mmG", ctx.n("mmG"))
                for i in range(8):
                    ctx.inc(sy.dma_start(out=xnTs[:, i * 512:(i + 1) * 512],
                                         in_=xnT_i[i * 128:(i + 1) * 128, c0:c0 + 512]), "xc", 16)
                ctx.wait(te, "xc", ctx.n("xc"))
                prodp = getattr(ctx, "_prodp", 0)
                for ft in range(16):
                    fb = 2 * (ft % 2)
                    if ft >= 1:
                        ctx.wait(te, "p", prodp_hist[-1])
                    for kt in range(8):
                        mm = te.matmul(bank(fb), gws[:, kt * FE + ft * 128: kt * FE + (ft + 1) * 128],
                                       xnTs[:, kt * 512:(kt + 1) * 512], start=(kt == 0), stop=(kt == 7))
                    ctx.inc(mm, "mmG")
                    for kt in range(8):
                        mm = te.matmul(bank(fb + 1), uws[:, kt * FE + ft * 128: kt * FE + (ft + 1) * 128],
                                       xnTs[:, kt * 512:(kt + 1) * 512], start=(kt == 0), stop=(kt == 7))
                    ctx.inc(mm, "mmG")
                    ctx.wait(a, "mmG", ctx.n("mmG"))
                    ctx.ch(a)(a.activation(out=sil_f[ft % 2][:, :], in_=bank(fb), func=AF.Silu))
                    ctx.wait(v, "p")
                    _ins = v.tensor_tensor(pr[:, ft * 512:(ft + 1) * 512],
                                           sil_f[ft % 2][:, :], bank(fb + 1), op=ALU.mult)
                    nnn = ctx.inc(_ins, "p")
                    prodp_hist.append(nnn)
                ctx.wait(te, "p", ctx.n("p"))
                for ts_ in range(4):
                    tt = (c0 // 128) + ts_
                    ttl = tt - th * 8
                    for oc in range(2):
                        pd_b = bank(4 + oc)
                        for ft in range(16):
                            mm = te.matmul(pd_b, pr[:, ft * 512 + ts_ * 128: ft * 512 + ts_ * 128 + 128],
                                           dws[:, ft * DM + oc * 512: ft * DM + (oc + 1) * 512],
                                           start=(ft == 0), stop=(ft == 15))
                        ctx.inc(mm, "mmD")
                    ctx.wait(v, "mmD", ctx.n("mmD"))
                    for oc in range(2):
                        ctx.ch(v)(v.scalar_tensor_tensor(moe[:, ttl * DM + oc * 512: ttl * DM + (oc + 1) * 512],
                                                         bank(4 + oc), cwt[:, tt * 9 + e: tt * 9 + e + 1],
                                                         moe[:, ttl * DM + oc * 512: ttl * DM + (oc + 1) * 512],
                                                         op0=ALU.mult, op1=ALU.add))
                    nmoe = ctx.n("p")
                    ctx.wait(te, "p", nmoe)
        for ttl in range(8):
            tt = th * 8 + ttl
            t0 = tt * 128
            ctx.wait(sy, "p", ctx.n("p"))
            ctx.wait(sy, "cmb", max(0, ctx.n("cmb")))
            ctx.inc(sy.dma_start(out=xts[:, :], in_=x1[t0:t0 + 128, :]), "xl1", 16)
            ctx.inc(sy.dma_start(out=sts[:, :], in_=s1[t0:t0 + 128, :]), "xl1", 16)
            ctx.wait(a, "xl1", ctx.n("xl1"))
            ctx.ch(a)(a.activation(out=gsil[:, :], in_=moe[:, ttl * DM:(ttl + 1) * DM], func=AF.Silu))
            ctx.wait(v, "xl1", ctx.n("xl1"))
            ctx.ch(v)(v.tensor_mul(x2s[:, :], sts[:, :], gsil[:, :]))
            nx2 = ctx.ch(v)(v.tensor_add(x2s[:, :], x2s[:, :], xts[:, :]))
            ctx.wait(sy, "p", nx2)
            ctx.inc(sy.dma_start(out=x2_o[t0:t0 + 128, :], in_=x2s[:, :]), "cmb", 16)
            ctx.wait(sy, "cmb")
        ctx.bar()
    ctx.drain()
    ctx.close()
    return nc




def build_l2a():
    ctx = Ctx("l2a")
    nc = ctx.nc
    x = nc.dram_tensor("x", [SQ, DM], F32, kind="ExternalInput").ap()
    WkT = nc.dram_tensor("WkT", [DM, DM], F32, kind="ExternalInput").ap()
    WqT = nc.dram_tensor("WqT", [DM, DM], F32, kind="ExternalInput").ap()
    WvT = nc.dram_tensor("WvT", [DM, DM], F32, kind="ExternalInput").ap()
    WaT = nc.dram_tensor("WaT", [DM, DM], F32, kind="ExternalInput").ap()
    WbT = nc.dram_tensor("WbT", [DM, DM], F32, kind="ExternalInput").ap()
    ident = nc.dram_tensor("ident", [128, 128], BF16, kind="ExternalInput").ap()
    xnT_o = nc.dram_tensor("xnT", [DM, SQ], BF16, kind="ExternalOutput").ap()
    s2_o = nc.dram_tensor("s2", [SQ, DM], BF16, kind="ExternalOutput").ap()

    v, a, te, sy, gp = nc.vector, nc.scalar, nc.tensor, nc.sync, nc.gpsimd
    wbuf = ctx.sb("wbuf2", [128, 3 * 8 * DM], BF16)
    wq = wbuf[:, 0 * 8 * DM:1 * 8 * DM]
    wk = wbuf[:, 1 * 8 * DM:2 * 8 * DM]
    wa = wbuf[:, 2 * 8 * DM:3 * 8 * DM]
    wv = wbuf[:, 0 * 8 * DM:1 * 8 * DM]
    wb = wbuf[:, 1 * 8 * DM:2 * 8 * DM]
    idt = ctx.sb("idt2", [128, 128], BF16)
    xts = [ctx.sb(f"xts2{i}", [128, DM], F32) for i in range(2)]
    xn = ctx.sb("xn2", [128, DM], BF16)
    xnT = ctx.sb("xnT2", [128, 8 * SQ], BF16)
    qT = ctx.sb("qT2", [128, 8 * SQ], BF16)
    kTc = ctx.sb("kTc2", [128, 8 * 512], BF16)
    akT = ctx.sb("akT2", [128, 8 * SQ], BF16)
    vTc = kTc
    u_t = ctx.sb("u_t2", [128, 16 * DM], BF16)
    probs = ctx.sb("probs2", [128, SQ], BF16)
    pT = ctx.sb("pT2", [128, 128], BF16)
    srow = ctx.sb("srow2", [128, DM], BF16)
    mx = ctx.sb("mx2", [128, 4], F32)
    mxs = ctx.sb("mxs2", [128, 1], F32)
    negms = ctx.sb("negms2", [128, 1], F32)
    den = ctx.sb("den2", [128, 4], F32)
    dsum = ctx.sb("dsum2", [128, 1], F32)
    scr = {k: ctx.sb(k + "_2a", [128, 1], F32) for k in ("mu", "varsum", "rstd", "t1", "t2")}
    scr["cen"] = ctx.sb("cen_2a", [128, DM], F32)
    scr["sq"] = probs.ap().bitcast(F32)[:, 0:DM]
    psP = ctx.ps("psP2a", [128, 3584], F32)
    psT = ctx.ps("psT2a", [128, 128], BF16)

    def bank(k):
        return psP[:, k * 512:(k + 1) * 512]

    ctx.constcol(1e-5)
    ctx.inc(sy.dma_start(out=idt[:, :], in_=ident[:, :]), "wld", 16)
    for w_hbm, w_sb in ((WqT, wq), (WkT, wk), (WaT, wa)):
        for kt in range(8):
            ctx.inc(gp.dma_start(out=w_sb[:, kt * DM:(kt + 1) * DM], in_=w_hbm[kt * 128:(kt + 1) * 128, :]), "gld", 16)
    ctx.inc(sy.dma_start(out=xts[0][:, :], in_=x[0:128, :]), "xld0", 16)
    ctx.wait(sy, "wld")
    ctx.wait(gp, "gld")
    ctx.wait(sy, "xld0")
    ctx.bar()

    for it in range(NT):
        bi = it % 2
        t0 = it * 128
        if it + 1 < NT:
            ctx.inc(sy.dma_start(out=xts[1 - bi][:, :], in_=x[t0 + 128:t0 + 256, :]), f"xld{1 - bi}", 16)
        ctx.wait(v, f"xld{bi}", 16 * (it // 2 + 1))
        n = emit_ln(ctx, xts[bi], 128, DM, xn, scr, eps=1e-5)
        ctx.wait(te, "p", n)
        for ib in range(8):
            ctx.wait(te, "tpe", ctx.n("tpe"))
            ctx.inc(te.transpose(psT[:, :], xn[:, ib * 128:(ib + 1) * 128], idt[:, :]), "tp")
            ctx.wait(a, "tp", ctx.n("tp"))
            ctx.inc(a.activation(out=xnT[:, ib * SQ + t0: ib * SQ + t0 + 128], in_=psT[:, :], func=AF.Identity), "tpe")
        ctx.wait(sy, "tpe", ctx.n("tpe"))
        for ib in range(8):
            ctx.inc(sy.dma_start(out=xnT_o[ib * 128:(ib + 1) * 128, t0:t0 + 128],
                                 in_=xnT[:, ib * SQ + t0: ib * SQ + t0 + 128]), "st", 16)
    ctx.wait(sy, "st")
    ctx.bar()

    for tc in range(4):
        c0 = tc * 512
        for ot in range(8):
            pb = ot % 2
            ctx.wait(te, "evq", ctx.n("evq") - 1)
            for kt in range(8):
                mm = te.matmul(bank(pb), wq[:, kt * DM + ot * 128: kt * DM + ot * 128 + 128],
                               xnT[:, kt * SQ + c0: kt * SQ + c0 + 512], start=(kt == 0), stop=(kt == 7))
            ctx.inc(mm, "mmq")
            ctx.wait(a, "mmq", ctx.n("mmq"))
            ctx.inc(a.activation(out=qT[:, ot * SQ + c0: ot * SQ + c0 + 512], in_=bank(pb), func=AF.Identity), "evq")
        for ot in range(8):
            pb = 2 + ot % 2
            ctx.wait(te, "evk", ctx.n("evk") - 1)
            for kt in range(8):
                mm = te.matmul(bank(pb), wk[:, kt * DM + ot * 128: kt * DM + ot * 128 + 128],
                               xnT[:, kt * SQ + c0: kt * SQ + c0 + 512], start=(kt == 0), stop=(kt == 7))
            ctx.inc(mm, "mmk")
            ctx.wait(a, "mmk", ctx.n("mmk"))
            ctx.inc(a.activation(out=kTc[:, ot * 512:(ot + 1) * 512], in_=bank(pb), func=AF.Identity), "evk")
        ctx.wait(te, "evk", ctx.n("evk"))
        for ot in range(8):
            pb = 4 + ot % 2
            ctx.wait(te, "evak", ctx.n("evak") - 1)
            for kt in range(8):
                mm = te.matmul(bank(pb), wa[:, kt * DM + ot * 128: kt * DM + ot * 128 + 128],
                               kTc[:, kt * 512:(kt + 1) * 512], start=(kt == 0), stop=(kt == 7))
            ctx.inc(mm, "mmak")
            ctx.wait(a, "mmak", ctx.n("mmak"))
            ctx.inc(a.activation(out=akT[:, ot * SQ + c0: ot * SQ + c0 + 512], in_=bank(pb), func=AF.Identity), "evak")
        pass
    ctx.wait(gp, "mmak", ctx.n("mmak"))
    ctx.wait(gp, "mmq", ctx.n("mmq"))
    for w_hbm, w_sb in ((WvT, wv), (WbT, wb)):
        for kt in range(8):
            ctx.inc(gp.dma_start(out=w_sb[:, kt * DM:(kt + 1) * DM], in_=w_hbm[kt * 128:(kt + 1) * 128, :]), "gld2", 16)
    ctx.wait(te, "gld2", ctx.n("gld2"))
    for tc in range(4):
        c0 = tc * 512
        for ot in range(8):
            pb = ot % 2
            ctx.wait(te, "evv", ctx.n("evv") - 1)
            for kt in range(8):
                mm = te.matmul(bank(pb), wv[:, kt * DM + ot * 128: kt * DM + ot * 128 + 128],
                               xnT[:, kt * SQ + c0: kt * SQ + c0 + 512], start=(kt == 0), stop=(kt == 7))
            ctx.inc(mm, "mmv")
            ctx.wait(a, "mmv", ctx.n("mmv"))
            ctx.inc(a.activation(out=vTc[:, ot * 512:(ot + 1) * 512], in_=bank(pb), func=AF.Identity), "evv")
        ctx.wait(te, "evv", ctx.n("evv"))
        for ts_ in range(4):
            tt = tc * 4 + ts_
            for oc in range(2):
                pb = 2 + oc
                ctx.wait(te, "evu", ctx.n("evu") - 1)
                for kt in range(8):
                    mm = te.matmul(bank(pb), vTc[:, kt * 512 + ts_ * 128: kt * 512 + ts_ * 128 + 128],
                                   wb[:, kt * DM + oc * 512: kt * DM + (oc + 1) * 512],
                                   start=(kt == 0), stop=(kt == 7))
                ctx.inc(mm, "mmu")
                ctx.wait(a, "mmu", ctx.n("mmu"))
                ctx.inc(a.activation(out=u_t[:, tt * DM + oc * 512: tt * DM + (oc + 1) * 512],
                                     in_=bank(pb), func=AF.Identity), "evu")
    ctx.bar()

    for it in range(NT):
        t0 = it * 128
        for kc in range(4):
            ps = bank(kc)
            for hb in range(8):
                mm = te.matmul(ps, qT[:, hb * SQ + t0: hb * SQ + t0 + 128],
                               akT[:, hb * SQ + kc * 512: hb * SQ + (kc + 1) * 512],
                               start=(hb == 0), stop=(hb == 7))
            ctx.inc(mm, "mmS")
        ctx.wait(v, "mmS", ctx.n("mmS"))
        for kc in range(4):
            ctx.ch(v)(v.tensor_reduce(mx[:, kc:kc + 1], bank(kc), axis=AX.X, op=ALU.max))
        ctx.ch(v)(v.tensor_reduce(mxs[:, 0:1], mx[:, 0:4], axis=AX.X, op=ALU.max))
        ctx.ch(v)(v.tensor_scalar_mul(negms[:, 0:1], mxs[:, 0:1], -1.0))
        for kc in range(4):
            ctx.ch(a)(a.activation(out=probs[:, kc * 512:(kc + 1) * 512], in_=bank(kc),
                                   func=AF.Exp, scale=1.0, bias=negms[:, 0:1]))
            ctx.ch(v)(v.reduce_sum(den[:, kc:kc + 1], probs[:, kc * 512:(kc + 1) * 512], axis=AX.X))
        ctx.ch(v)(v.reduce_sum(dsum[:, 0:1], den[:, 0:4], axis=AX.X))
        nsf = ctx.ch(v)(v.reciprocal(dsum[:, 0:1], dsum[:, 0:1]))
        for kt in range(16):
            if kt == 0:
                ctx.wait(te, "p", nsf)
            ctx.wait(te, "tpe", ctx.n("tpe"))
            ctx.inc(te.transpose(psT[:, :], probs[:, kt * 128:(kt + 1) * 128], idt[:, :]), "tp")
            ctx.wait(a, "tp", ctx.n("tp"))
            ctx.inc(a.activation(out=pT[:, :], in_=psT[:, :], func=AF.Identity), "tpe")
            ctx.wait(te, "tpe", ctx.n("tpe"))
            for oc in range(2):
                mm = te.matmul(bank(4 + oc), pT[:, :], u_t[:, kt * DM + oc * 512: kt * DM + (oc + 1) * 512],
                               start=(kt == 0), stop=(kt == 15))
            ctx.inc(mm, "mmA")
        ctx.wait(v, "mmA", ctx.n("mmA"))
        ctx.wait(v, "st1", ctx.n("st1"))
        ctx.ch(v)(v.tensor_scalar(srow[:, 0:512], bank(4), dsum[:, 0:1], None, op0=ALU.mult))
        nfin = ctx.ch(v)(v.tensor_scalar(srow[:, 512:1024], bank(5), dsum[:, 0:1], None, op0=ALU.mult))
        ctx.wait(sy, "p", nfin)
        ctx.inc(sy.dma_start(out=s2_o[t0:t0 + 128, :], in_=srow[:, :]), "st1", 16)
        ctx.wait(sy, "st1")
    ctx.drain()
    ctx.close()
    return nc


def build_l2b():
    ctx = Ctx("l2b")
    nc = ctx.nc
    FH = 4096
    x = nc.dram_tensor("x", [SQ, DM], F32, kind="ExternalInput").ap()
    s2 = nc.dram_tensor("s2", [SQ, DM], BF16, kind="ExternalInput").ap()
    xnT_i = nc.dram_tensor("xnT", [DM, SQ], BF16, kind="ExternalInput").ap()
    w1T = nc.dram_tensor("w1T", [DM, FH], F32, kind="ExternalInput").ap()
    w2T = nc.dram_tensor("w2T", [DM, FH], F32, kind="ExternalInput").ap()
    h1T = nc.dram_tensor("h1T", [FH, FH], F32, kind="ExternalInput").ap()
    h2T = nc.dram_tensor("h2T", [FH, FH], F32, kind="ExternalInput").ap()
    w3T = nc.dram_tensor("w3T", [FH, DM], F32, kind="ExternalInput").ap()
    s1c_i = nc.dram_tensor("s1c", [128, 32], F32, kind="ExternalInput").ap()
    s2c_i = nc.dram_tensor("s2c", [128, 32], F32, kind="ExternalInput").ap()
    h1c_i = nc.dram_tensor("h1c", [128, 32], F32, kind="ExternalInput").ap()
    h2c_i = nc.dram_tensor("h2c", [128, 32], F32, kind="ExternalInput").ap()
    s3b_i = nc.dram_tensor("s3b", [128, DM], F32, kind="ExternalInput").ap()
    x3_o = nc.dram_tensor("x3", [SQ, DM], F32, kind="ExternalOutput").ap()
    w1s = nc.dram_tensor("w1s_scr", [32 * DM, 128], BF16, kind="Internal").ap()
    w2s = nc.dram_tensor("w2s_scr", [32 * DM, 128], BF16, kind="Internal").ap()
    h1s = nc.dram_tensor("h1s_scr", [32 * FH, 128], BF16, kind="Internal").ap()
    h2s = nc.dram_tensor("h2s_scr", [32 * FH, 128], BF16, kind="Internal").ap()
    w3s = nc.dram_tensor("w3s_scr", [FH, DM], BF16, kind="Internal").ap()

    v, a, te, sy, gp = nc.vector, nc.scalar, nc.tensor, nc.sync, nc.gpsimd
    wload = [ctx.sb(f"wload2{i}", [128, FH], BF16) for i in range(2)]
    xnTs = ctx.sb("xnTs2b", [128, 8 * 512], BF16)
    hfm = ctx.sb("hfm", [128, 32 * 512], BF16)
    h1fm = ctx.sb("h1fm", [128, 32 * 512], BF16)
    wcb = [ctx.sb(f"wcb{i}", [128, 32 * 128], BF16) for i in range(2)]
    s1c = ctx.sb("s1cs", [128, 32], F32)
    s2cs = ctx.sb("s2cs", [128, 32], F32)
    h1cs = ctx.sb("h1cs", [128, 32], F32)
    h2cs = ctx.sb("h2cs", [128, 32], F32)
    s3b = ctx.sb("s3bs", [128, DM], F32)
    silf = [ctx.sb(f"silf2{i}", [128, 512], F32) for i in range(2)]
    xts = ctx.sb("xts2b", [128, DM], F32)
    sts = ctx.sb("sts2b", [128, DM], BF16)
    gsil = ctx.sb("gsil2b", [128, DM], F32)
    x3s = ctx.sb("x3s", [128, DM], F32)
    psP = ctx.ps("psP2b", [128, 4096], F32)

    def bank(k):
        return psP[:, k * 512:(k + 1) * 512]

    for t, src in ((s1c, s1c_i), (s2cs, s2c_i), (h1cs, h1c_i), (h2cs, h2c_i), (s3b, s3b_i)):
        ctx.inc(sy.dma_start(out=t[:, :], in_=src[:, :]), "wld", 16)
    cvi = 0
    for wt, scr_t, ntile, wid, colblk in ((w1T, w1s, 8, FH, True), (w2T, w2s, 8, FH, True),
                                          (h1T, h1s, 32, FH, True), (h2T, h2s, 32, FH, True),
                                          (w3T, w3s, 32, DM, False)):
        nblk = wid // 128
        for kt in range(ntile):
            pb = cvi % 2
            ctx.wait(gp, f"wcs{pb}", 16 * (cvi // 2))
            ctx.inc(gp.dma_start(out=wload[pb][:, 0:wid], in_=wt[kt * 128:(kt + 1) * 128, :]), f"wcv{pb}", 16)
            ctx.wait(sy, f"wcv{pb}", 16 * (cvi // 2 + 1))
            if colblk:
                ctx.inc(sy.dma_start(out=scr_t.rearrange("(t f) c -> t f c", t=nblk)[:, kt * 128:(kt + 1) * 128, :].rearrange("t f c -> f t c"),
                                     in_=wload[pb].ap().rearrange("p (t c) -> p t c", t=nblk)), f"wcs{pb}", 16)
            else:
                ctx.inc(sy.dma_start(out=scr_t[kt * 128:(kt + 1) * 128, :], in_=wload[pb][:, 0:wid]), f"wcs{pb}", 16)
            cvi += 1
    ctx.wait(sy, "wcs0")
    ctx.wait(sy, "wcs1")
    ctx.wait(gp, "wcv0")
    ctx.wait(gp, "wcv1")
    ctx.wait(sy, "wld")
    ctx.bar()

    for tc in range(4):
        c0 = tc * 512
        ctx.wait(sy, "mm1", ctx.n("mm1"))
        for i in range(8):
            ctx.inc(sy.dma_start(out=xnTs[:, i * 512:(i + 1) * 512],
                                 in_=xnT_i[i * 128:(i + 1) * 128, c0:c0 + 512]), "xc", 16)
        ctx.wait(te, "xc", ctx.n("xc"))
        ctx.wait(te, "p", ctx.n("p"))
        l2p1 = [0, 0]
        ctx.wait(sy, "mm3", ctx.n("mm3"))
        for ft in range(32):
            pb2 = ft % 2
            fb = 2 * (ft % 2)
            ctx.wait(sy, "mm1", max(0, ctx.n("mm1") - 2))
            ctx.inc(sy.dma_start(out=wcb[pb2][:, 0:1024].rearrange("p (t c) -> p t c", t=8),
                                 in_=w1s[ft * DM:(ft + 1) * DM, :].rearrange("(k p) c -> p k c", k=8)), f"sw{pb2}", 16)
            ctx.inc(sy.dma_start(out=wcb[pb2][:, 1024:2048].rearrange("p (t c) -> p t c", t=8),
                                 in_=w2s[ft * DM:(ft + 1) * DM, :].rearrange("(k p) c -> p k c", k=8)), f"sw{pb2}", 16)
            ctx.wait(te, f"sw{pb2}", ctx.n(f"sw{pb2}"))
            if ft >= 2:
                ctx.wait(te, "p", l2p1[-2])
            for kt in range(8):
                mm = te.matmul(bank(fb), wcb[pb2][:, kt * 128:(kt + 1) * 128],
                               xnTs[:, kt * 512:(kt + 1) * 512], start=(kt == 0), stop=(kt == 7))
            ctx.inc(mm, "mm1")
            for kt in range(8):
                mm = te.matmul(bank(fb + 1), wcb[pb2][:, 1024 + kt * 128: 1024 + (kt + 1) * 128],
                               xnTs[:, kt * 512:(kt + 1) * 512], start=(kt == 0), stop=(kt == 7))
            ctx.inc(mm, "mm1")
            ctx.wait(v, "mm1", ctx.n("mm1"))
            ctx.ch(v)(v.tensor_scalar(silf[ft % 2][:, :], bank(fb), s1c[:, ft:ft + 1], None, op0=ALU.mult))
            ctx.ch(a)(a.activation(out=silf[ft % 2][:, :], in_=silf[ft % 2][:, :], func=AF.Silu))
            ctx.ch(v)(v.tensor_scalar(hfm[:, ft * 512:(ft + 1) * 512], bank(fb + 1), s2cs[:, ft:ft + 1], None, op0=ALU.mult))
            nnn = ctx.ch(v)(v.tensor_mul(hfm[:, ft * 512:(ft + 1) * 512],
                                         hfm[:, ft * 512:(ft + 1) * 512], silf[ft % 2][:, :]))
            l2p1.append(nnn)
        for stage, (scr_t, scol, dst) in enumerate(((h1s, h1cs, h1fm), (h2s, h2cs, hfm))):
            src = hfm if stage == 0 else h1fm
            ctx.wait(te, "p", ctx.n("p"))
            ctx.wait(sy, "mm1", ctx.n("mm1"))
            ctx.wait(sy, "mmH", ctx.n("mmH"))
            l2p2 = [0, 0]
            for f2 in range(32):
                pb2 = f2 % 2
                ctx.wait(sy, "mmH", max(0, ctx.n("mmH") - 1))
                ctx.inc(sy.dma_start(out=wcb[pb2].ap().rearrange("p (t c) -> p t c", t=32),
                                     in_=scr_t[f2 * FH:(f2 + 1) * FH, :].rearrange("(k p) c -> p k c", k=32)), f"hw{pb2}", 16)
                ctx.wait(te, f"hw{pb2}", ctx.n(f"hw{pb2}"))
                fb = 4 + (f2 % 2)
                if f2 >= 2:
                    ctx.wait(te, "p", l2p2[-2])
                for kt in range(32):
                    mm = te.matmul(bank(fb), wcb[pb2][:, kt * 128:(kt + 1) * 128],
                                   src[:, kt * 512:(kt + 1) * 512], start=(kt == 0), stop=(kt == 31))
                ctx.inc(mm, "mmH")
                ctx.wait(v, "mmH", ctx.n("mmH"))
                ctx.ch(v)(v.tensor_scalar(dst[:, f2 * 512:(f2 + 1) * 512], bank(fb),
                                          scol[:, f2:f2 + 1], None, op0=ALU.mult))
                nnn = ctx.ch(a)(a.activation(out=dst[:, f2 * 512:(f2 + 1) * 512],
                                             in_=dst[:, f2 * 512:(f2 + 1) * 512], func=AF.Gelu))
                l2p2.append(nnn)
        ctx.wait(te, "p", ctx.n("p"))
        ctx.wait(sy, "mmH", ctx.n("mmH"))
        for kg in range(8):
            pb2 = kg % 2
            ctx.wait(sy, "mm3", max(0, ctx.n("mm3") - 4))
            ctx.inc(sy.dma_start(out=wcb[pb2][:, 0:4096].rearrange("p (t c) -> p t c", t=4),
                                 in_=w3s[kg * 512:(kg + 1) * 512, :].rearrange("(t p) c -> p t c", t=4)), f"w3w{pb2}", 16)
            ctx.wait(te, f"w3w{pb2}", ctx.n(f"w3w{pb2}"))
            for kt in range(4):
                kk = kg * 4 + kt
                for ts_ in range(4):
                    for oc in range(2):
                        mm = te.matmul(bank(ts_ * 2 + oc),
                                       hfm[:, kk * 512 + ts_ * 128: kk * 512 + ts_ * 128 + 128],
                                       wcb[pb2][:, kt * 1024 + oc * 512: kt * 1024 + (oc + 1) * 512],
                                       start=(kk == 0), stop=(kk == 31))
                    ctx.inc(mm, "mm3")
        ctx.wait(v, "mm3", ctx.n("mm3"))
        for ts_ in range(4):
            tt = tc * 4 + ts_
            t0 = tt * 128
            ctx.wait(sy, "p", ctx.n("p"))
            ctx.wait(sy, "cmb", ctx.n("cmb"))
            ctx.inc(sy.dma_start(out=xts[:, :], in_=x[t0:t0 + 128, :]), "xl3", 16)
            ctx.inc(sy.dma_start(out=sts[:, :], in_=s2[t0:t0 + 128, :]), "xl3", 16)
            ctx.wait(v, "xl3", ctx.n("xl3"))
            ctx.ch(v)(v.tensor_mul(gsil[:, 0:512], bank(ts_ * 2), s3b[:, 0:512]))
            ctx.ch(v)(v.tensor_mul(gsil[:, 512:1024], bank(ts_ * 2 + 1), s3b[:, 512:1024]))
            ctx.ch(a)(a.activation(out=gsil[:, :], in_=gsil[:, :], func=AF.Silu))
            ctx.ch(v)(v.tensor_mul(x3s[:, :], sts[:, :], gsil[:, :]))
            nx3 = ctx.ch(v)(v.tensor_add(x3s[:, :], x3s[:, :], xts[:, :]))
            ctx.wait(sy, "p", nx3)
            ctx.inc(sy.dma_start(out=x3_o[t0:t0 + 128, :], in_=x3s[:, :]), "cmb", 16)
    ctx.drain()
    ctx.close()
    return nc




def build_l3a():
    ctx = Ctx("l3a")
    nc = ctx.nc
    G4 = 4096
    x = nc.dram_tensor("x", [SQ, DM], F32, kind="ExternalInput").ap()
    W4x = nc.dram_tensor("W4x", [DM, G4], F32, kind="ExternalInput").ap()
    b4b = nc.dram_tensor("b4b", [128, G4], F32, kind="ExternalInput").ap()
    ident = nc.dram_tensor("ident", [128, 128], BF16, kind="ExternalInput").ap()
    xn_o = nc.dram_tensor("xn", [SQ, DM], BF16, kind="ExternalOutput").ap()
    xg_o = nc.dram_tensor("xg", [SQ, G4], BF16, kind="ExternalOutput").ap()

    v, a, te, sy, gp = nc.vector, nc.scalar, nc.tensor, nc.sync, nc.gpsimd
    w4 = ctx.sb("w4", [128, 8 * G4], BF16)
    b4 = ctx.sb("b4", [128, G4], F32)
    idt = ctx.sb("idt3", [128, 128], BF16)
    xts = [ctx.sb(f"xts3{i}", [128, DM], F32) for i in range(2)]
    xn = ctx.sb("xn3", [128, DM], BF16)
    xnT = ctx.sb("xnT3", [128, DM], BF16)
    xg = ctx.sb("xg3", [128, G4], BF16)
    scr = {k: ctx.sb(k + "_3a", [128, 1], F32) for k in ("mu", "varsum", "rstd", "t1", "t2")}
    scr["cen"] = ctx.sb("cen_3a", [128, DM], F32)
    scr["sq"] = ctx.sb("sq_3a", [128, DM], F32)
    psP = ctx.ps("psP3a", [128, 3584], F32)
    psT = ctx.ps("psT3a", [128, 128], BF16)

    def bank(k):
        return psP[:, k * 512:(k + 1) * 512]

    ctx.constcol(1e-5)
    ctx.inc(sy.dma_start(out=idt[:, :], in_=ident[:, :]), "wld", 16)
    ctx.inc(sy.dma_start(out=b4[:, :], in_=b4b[:, :]), "wld", 16)
    for kt in range(8):
        ctx.inc(gp.dma_start(out=w4[:, kt * G4:(kt + 1) * G4], in_=W4x[kt * 128:(kt + 1) * 128, :]), "gld", 16)
    ctx.inc(sy.dma_start(out=xts[0][:, :], in_=x[0:128, :]), "xld0", 16)
    ctx.wait(sy, "wld")
    ctx.wait(gp, "gld")
    ctx.wait(sy, "xld0")
    ctx.bar()

    for it in range(NT):
        bi = it % 2
        t0 = it * 128
        if it + 1 < NT:
            ctx.inc(sy.dma_start(out=xts[1 - bi][:, :], in_=x[t0 + 128:t0 + 256, :]), f"xld{1 - bi}", 16)
        ctx.wait(v, f"xld{bi}", 16 * (it // 2 + 1))
        ctx.wait(v, "st", ctx.n("st"))
        ctx.wait(a, "st", ctx.n("st"))
        n = emit_ln(ctx, xts[bi], 128, DM, xn, scr, eps=1e-5)
        ctx.wait(sy, "p", n)
        ctx.inc(sy.dma_start(out=xn_o[t0:t0 + 128, :], in_=xn[:, :]), "st", 16)
        ctx.wait(te, "p", n)
        emit_transposes(ctx, xn, 8, psT, xnT, idt)
        ctx.wait(te, "tpe", ctx.n("tpe"))
        for oc in range(8):
            ps = bank(oc % 2)
            ctx.wait(te, "ev", ctx.n("ev") - 1)
            for kt in range(8):
                mm = te.matmul(ps, xnT[:, kt * 128:(kt + 1) * 128],
                               w4[:, kt * G4 + oc * 512: kt * G4 + (oc + 1) * 512],
                               start=(kt == 0), stop=(kt == 7))
            ctx.inc(mm, "mm")
            ctx.wait(v, "mm", ctx.n("mm"))
            ctx.wait(v, "st", ctx.n("st") - 16)
            ctx.inc(v.tensor_tensor(xg[:, oc * 512:(oc + 1) * 512], ps,
                                    b4[:, oc * 512:(oc + 1) * 512], op=ALU.add), "ev")
        ctx.wait(sy, "ev", ctx.n("ev"))
        ctx.inc(sy.dma_start(out=xg_o[t0:t0 + 128, :], in_=xg[:, :]), "st", 16)
    ctx.drain()
    ctx.close()
    return nc


def build_l3b(NITER=256, DBG=False):
    ctx = Ctx("l3b")
    nc = ctx.nc
    G4 = 4096
    xgp = nc.dram_tensor("xgp", [514, 8, 4 * G4], BF16, kind="ExternalInput").ap()
    whh4T = nc.dram_tensor("whh4T", [DM, G4], F32, kind="ExternalInput").ap()
    wegN = nc.dram_tensor("wegN", [DM, DM], F32, kind="ExternalInput").ap()
    begb = nc.dram_tensor("begb", [8, DM], F32, kind="ExternalInput").ap()
    i8b = nc.dram_tensor("i8b", [8, 8], BF16, kind="ExternalInput").ap()
    bmask = nc.dram_tensor("bmask", [8, 1], BF16, kind="ExternalInput").ap()
    hs_o = nc.dram_tensor("hs", [512, 4, DM], BF16, kind="ExternalOutput").ap()

    v, a, te, sy, gp = nc.vector, nc.scalar, nc.tensor, nc.sync, nc.gpsimd
    whh = ctx.sb("whh4", [128, 8 * G4], BF16)
    weg = ctx.sb("weg", [128, 8 * DM], BF16)
    xgt = [ctx.sb(f"xgt{i}", [8, 4 * G4], BF16) for i in range(2)]
    hT_bf = ctx.sb("hT3", [128, 64], BF16)
    gT_bf = ctx.sb("gT3", [128, 64], BF16)
    hrow = ctx.sb("hrow3", [8, DM], F32)
    crow = ctx.sb("crow3", [8, DM], F32)
    nrow = ctx.sb("nrow3", [8, DM], F32)
    hb_bf = ctx.sb("hb3", [8, DM], BF16)
    gb_bf = ctx.sb("gb3", [8, DM], BF16)
    beg = ctx.sb("beg3", [8, DM], F32)
    i8s = ctx.sb("i8s3", [8, 8], BF16)
    bms = ctx.sb("bms3", [8, 1], BF16)
    hstage = ctx.sb("hstage3", [1, 4 * DM], BF16)
    g_g = ctx.sb("g_g3", [8, DM], F32)
    i_g = ctx.sb("i_g3", [8, DM], F32)
    f_g = ctx.sb("f_g3", [8, DM], F32)
    o_g = ctx.sb("o_g3", [8, DM], F32)
    e_g = ctx.sb("e_g3", [8, DM], F32)
    t_g = ctx.sb("t_g3", [8, DM], F32)
    t2g = ctx.sb("t2g3", [8, DM], F32)
    psP = ctx.ps("psPc3", [128, 4096], F32)

    def bank(k):
        return psP[:, k * 512:(k + 1) * 512]

    preload_set(nc, "sigmoid_and_others")
    for kt in range(8):
        ctx.inc(gp.dma_start(out=whh[:, kt * G4:(kt + 1) * G4], in_=whh4T[kt * 128:(kt + 1) * 128, :]), "gld", 16)
        ctx.inc(gp.dma_start(out=weg[:, kt * DM:(kt + 1) * DM], in_=wegN[kt * 128:(kt + 1) * 128, :]), "gld", 16)
    ctx.inc(sy.dma_start(out=beg[0:8, :], in_=begb[:, :]), "wld", 16)
    ctx.inc(sy.dma_start(out=i8s[:, :], in_=i8b[:, :]), "wld", 16)
    ctx.inc(sy.dma_start(out=bms[:, :], in_=bmask[:, :]), "wld", 16)
    ctx.inc(sy.dma_start(out=xgt[0][0:8, :], in_=xgp[0, :, :]), "xg0", 16)
    v.memset(hT_bf[:, :], 0.0)
    v.memset(hrow[0:8, :], 0.0)
    v.memset(crow[0:8, :], 0.0)
    v.memset(nrow[0:8, :], 0.0)
    ctx.wait(sy, "wld")
    ctx.wait(sy, "xg0")
    ctx.wait(gp, "gld")
    ctx.bar()

    loop_sems = ["mmA", "mmB", "mmZ", "mmT", "pp", "xga", "xgb", "hst"]
    for nm in loop_sems:
        ctx.sem(nm)
    semxg = {0: ctx.sem("xga"), 1: ctx.sem("xgb")}

    def emit_sub(par, st):
        for s_ in range(4):
            xt = xgt[par][0:8, s_ * G4:(s_ + 1) * G4]

            def TT(out, a_, b_, op, eng=None):
                e_ = eng or nc.vector
                e_.wait_ge(ctx.sem("pp"), st["pp"])
                ins = e_.tensor_tensor(out, a_, b_, op=op)
                st["pp"] += 1
                ins.then_inc(ctx.sem("pp"), 1)

            def ACT(out, in_, func, scale=1.0):
                nc.scalar.wait_ge(ctx.sem("pp"), st["pp"])
                ins = nc.scalar.activation(out=out, in_=in_, func=func, scale=scale)
                st["pp"] += 1
                ins.then_inc(ctx.sem("pp"), 1)

            if st["pp"] > 0:
                nc.tensor.wait_ge(ctx.sem("pp"), st["pp"])
            for oc in range(2):
                for kt in range(8):
                    mm = nc.tensor.matmul(bank(oc)[0:8, :], hT_bf[:, kt * 8:(kt + 1) * 8],
                                          whh[:, kt * G4 + oc * 512: kt * G4 + (oc + 1) * 512],
                                          start=(kt == 0), stop=(kt == 7))
            st["mmA"] += 1
            mm.then_inc(ctx.sem("mmA"), 1)
            for oc in range(2, 8):
                for kt in range(8):
                    mm = nc.tensor.matmul(bank(oc)[0:8, :], hT_bf[:, kt * 8:(kt + 1) * 8],
                                          whh[:, kt * G4 + oc * 512: kt * G4 + (oc + 1) * 512],
                                          start=(kt == 0), stop=(kt == 7))
            st["mmB"] += 1
            mm.then_inc(ctx.sem("mmB"), 1)
            nc.vector.wait_ge(ctx.sem("mmA"), st["mmA"])
            TT(t_g[0:8, 0:512], bank(0)[0:8, :], xt[:, 0:512], ALU.add)
            TT(t_g[0:8, 512:1024], bank(1)[0:8, :], xt[:, 512:1024], ALU.add)
            ACT(g_g[0:8, :], t_g[0:8, :], AF.Tanh)
            ACT(gb_bf[0:8, :], g_g[0:8, :], AF.Identity)
            nc.tensor.wait_ge(ctx.sem("pp"), st["pp"])
            for gg in range(8):
                mm = nc.tensor.matmul(bank(0)[:, gg * 8:(gg + 1) * 8], gb_bf[0:8, gg * 128:(gg + 1) * 128],
                                      i8s[0:8, :], start=True, stop=True)
            st["mmT"] += 1
            mm.then_inc(ctx.sem("mmT"), 1)
            nc.vector.wait_ge(ctx.sem("mmT"), st["mmT"])
            nc.vector.wait_ge(ctx.sem("pp"), st["pp"])
            ins = nc.vector.tensor_copy(gT_bf[:, :], bank(0)[:, 0:64])
            st["pp"] += 1
            ins.then_inc(ctx.sem("pp"), 1)
            nc.vector.wait_ge(ctx.sem("mmB"), st["mmB"])
            TT(t_g[0:8, 0:512], bank(2)[0:8, :], xt[:, 1024:1536], ALU.add)
            TT(t_g[0:8, 512:1024], bank(3)[0:8, :], xt[:, 1536:2048], ALU.add)
            ACT(i_g[0:8, :], t_g[0:8, :], AF.Sigmoid)
            TT(t_g[0:8, 0:512], bank(4)[0:8, :], xt[:, 2048:2560], ALU.add)
            TT(t_g[0:8, 512:1024], bank(5)[0:8, :], xt[:, 2560:3072], ALU.add)
            ACT(f_g[0:8, :], t_g[0:8, :], AF.Sigmoid)
            TT(t_g[0:8, 0:512], bank(6)[0:8, :], xt[:, 3072:3584], ALU.add)
            TT(t_g[0:8, 512:1024], bank(7)[0:8, :], xt[:, 3584:4096], ALU.add)
            ACT(o_g[0:8, :], t_g[0:8, :], AF.Sigmoid)
            nc.tensor.wait_ge(ctx.sem("pp"), st["pp"])
            for oc in range(2):
                for kt in range(8):
                    mm = nc.tensor.matmul(bank(2 + oc)[0:8, :], gT_bf[:, kt * 8:(kt + 1) * 8],
                                          weg[:, kt * DM + oc * 512: kt * DM + (oc + 1) * 512],
                                          start=(kt == 0), stop=(kt == 7))
            st["mmZ"] += 1
            mm.then_inc(ctx.sem("mmZ"), 1)
            nc.vector.wait_ge(ctx.sem("mmZ"), st["mmZ"])
            TT(t_g[0:8, 0:512], bank(2)[0:8, :], beg[0:8, 0:512], ALU.add)
            TT(t_g[0:8, 512:1024], bank(3)[0:8, :], beg[0:8, 512:1024], ALU.add)
            ACT(e_g[0:8, :], t_g[0:8, :], AF.Sigmoid)
            ACT(t2g[0:8, :], t_g[0:8, :], AF.Sigmoid, scale=-1.0)
            nc.vector.wait_ge(ctx.sem("pp"), st["pp"])
            ins = nc.vector.reciprocal(t2g[0:8, :], t2g[0:8, :])
            st["pp"] += 1
            ins.then_inc(ctx.sem("pp"), 1)
            TT(e_g[0:8, :], e_g[0:8, :], t2g[0:8, :], ALU.mult)
            TT(crow[0:8, :], f_g[0:8, :], crow[0:8, :], ALU.mult)
            TT(t_g[0:8, :], i_g[0:8, :], g_g[0:8, :], ALU.mult)
            TT(crow[0:8, :], crow[0:8, :], t_g[0:8, :], ALU.add)
            TT(nrow[0:8, :], f_g[0:8, :], nrow[0:8, :], ALU.mult)
            TT(t_g[0:8, :], i_g[0:8, :], e_g[0:8, :], ALU.mult)
            TT(nrow[0:8, :], nrow[0:8, :], t_g[0:8, :], ALU.add)
            nc.vector.wait_ge(ctx.sem("pp"), st["pp"])
            ins = nc.vector.reciprocal(t2g[0:8, :], nrow[0:8, :])
            st["pp"] += 1
            ins.then_inc(ctx.sem("pp"), 1)
            TT(hrow[0:8, :], o_g[0:8, :], crow[0:8, :], ALU.mult)
            TT(hrow[0:8, :], hrow[0:8, :], t2g[0:8, :], ALU.mult)
            ACT(hb_bf[0:8, :], hrow[0:8, :], AF.Identity)
            nc.tensor.wait_ge(ctx.sem("pp"), st["pp"])
            for gg in range(8):
                nc.tensor.matmul(bank(4)[:, gg * 8:(gg + 1) * 8], hb_bf[0:8, gg * 128:(gg + 1) * 128],
                                 i8s[0:8, :], start=True, stop=True)
            for half in range(2):
                mm = nc.tensor.matmul(bank(5 + half)[0:1, :], bms[0:8, 0:1],
                                      hb_bf[0:8, half * 512:(half + 1) * 512], start=True, stop=True)
            st["mmT"] += 1
            mm.then_inc(ctx.sem("mmT"), 1)
            nc.vector.wait_ge(ctx.sem("mmT"), st["mmT"])
            nc.vector.wait_ge(ctx.sem("pp"), st["pp"])
            ins = nc.vector.tensor_copy(hT_bf[:, :], bank(4)[:, 0:64])
            st["pp"] += 1
            ins.then_inc(ctx.sem("pp"), 1)
            for half in range(2):
                nc.vector.wait_ge(ctx.sem("pp"), st["pp"])
                if half == 0:
                    nc.vector.wait_ge(ctx.sem("hst"), st["hst"])
                ins = nc.vector.tensor_copy(hstage[0:1, s_ * DM + half * 512: s_ * DM + (half + 1) * 512],
                                            bank(5 + half)[0:1, :])
                st["pp"] += 1
                ins.then_inc(ctx.sem("pp"), 1)

    with nc.Fori(0, NITER) as it_:
        i2 = nc.sync.snap(it_)
        st = {"mmA": 0, "mmB": 0, "mmZ": 0, "mmT": 0, "pp": 0, "hst": 0}
        nc.sync.dma_start(out=xgt[1][0:8, :], in_=xgp[bass.ds(2 * i2 + 1, 1), :, :]).then_inc(semxg[1], 16)
        emit_sub(0, st)
        nc.sync.wait_ge(ctx.sem("pp"), st["pp"])
        st["hst"] += 16
        nc.sync.dma_start(out=hs_o[bass.ds(2 * i2, 1), :, :].rearrange("o b f -> o (b f)"),
                          in_=hstage[0:1, :]).then_inc(ctx.sem("hst"), 16)
        nc.sync.dma_start(out=xgt[0][0:8, :], in_=xgp[bass.ds(2 * i2 + 2, 1), :, :]).then_inc(semxg[0], 16)
        nc.sync.wait_ge(semxg[1], 16)
        nc.vector.wait_ge(semxg[1], 16)
        nc.tensor.wait_ge(semxg[1], 16)
        emit_sub(1, st)
        nc.sync.wait_ge(ctx.sem("pp"), st["pp"])
        st["hst"] += 16
        nc.sync.dma_start(out=hs_o[bass.ds(2 * i2 + 1, 1), :, :].rearrange("o b f -> o (b f)"),
                          in_=hstage[0:1, :]).then_inc(ctx.sem("hst"), 16)
        nc.sync.wait_ge(ctx.sem("hst"), st["hst"])
        nc.sync.wait_ge(semxg[0], 16)
        nc.all_engine_barrier()
        for nm in loop_sems:
            nc.gpsimd.sem_clear(ctx.sem(nm))
        nc.all_engine_barrier()
    ctx.bar()
    ctx.drain()
    ctx.close()
    return nc




def build_l3c(STAGE=99):
    ctx = Ctx("l3c")
    nc = ctx.nc
    FE = 2048
    CAP = 5120.0
    xn_i = nc.dram_tensor("xnsl", [SQ, DM], BF16, kind="ExternalInput").ap()
    qkvT = nc.dram_tensor("qkvT", [DM, 1536], F32, kind="ExternalInput").ap()
    projT = nc.dram_tensor("projT", [DM, DM], F32, kind="ExternalInput").ap()
    rw = nc.dram_tensor("rw", [DM, 8], F32, kind="ExternalInput").ap()
    ln1wb = nc.dram_tensor("ln1wb", [128, DM], F32, kind="ExternalInput").ap()
    ln1bb = nc.dram_tensor("ln1bb", [128, DM], F32, kind="ExternalInput").ap()
    ln2wb = nc.dram_tensor("ln2wb", [128, DM], F32, kind="ExternalInput").ap()
    ln2bb = nc.dram_tensor("ln2bb", [128, DM], F32, kind="ExternalInput").ap()
    mw1 = nc.dram_tensor("mw1", [8, DM, FE], F32, kind="ExternalInput").ap()
    mw2 = nc.dram_tensor("mw2", [8, FE, DM], F32, kind="ExternalInput").ap()
    tri_i = nc.dram_tensor("tri", [128, 128], BF16, kind="ExternalInput").ap()
    wsel_i = nc.dram_tensor("wsel", [64, 8], F32, kind="ExternalInput").ap()
    ident = nc.dram_tensor("ident", [128, 128], BF16, kind="ExternalInput").ap()
    g3_o = nc.dram_tensor("g3", [SQ, DM], BF16, kind="ExternalOutput").ap()
    mc_d = nc.dram_tensor("mc_d", [8, 8], F32, kind="Internal").ap()
    gall_d = nc.dram_tensor("gall_d", [64, 8], F32, kind="Internal").ap()
    offs_d = nc.dram_tensor("offs_d", [8, 8], F32, kind="Internal").ap()
    cnt0_d = nc.dram_tensor("cnt0_d", [8, 8], F32, kind="Internal").ap()

    v, a, te, sy, gp = nc.vector, nc.scalar, nc.tensor, nc.sync, nc.gpsimd
    A1 = ctx.sb("A1", [128, 24576], BF16)
    A2 = ctx.sb("A2", [128, 16384], BF16)
    A3 = ctx.sb("A3", [128, 8192], F32)
    h1t = ctx.sb("h1t", [128, 16384], BF16)
    qkw = ctx.sb("qkw", [128, 8 * 1536], BF16)
    pjw = qkw[:, 0:8 * DM]
    rww = ctx.sb("rww", [128, 8 * 8], BF16)
    w2d = [ctx.sb(f"w2d{i}", [128, DM], BF16) for i in range(2)]
    lnw = [ctx.sb(f"lnw{i}", [128, DM], F32) for i in range(4)]
    tri = ctx.sb("tris", [128, 128], BF16)
    wsel = ctx.sb("wsels", [64, 8], F32)
    idt = ctx.sb("idt3c", [128, 128], BF16)
    onescol = ctx.sb("onescol", [128, 1], BF16)
    lnt = ctx.sb("lnt3c", [128, DM], BF16)
    sc_f = ctx.sb("sc_f", [128, 1024], F32)
    sc_s = ctx.sb("sc_s", [128, 128], F32)
    m16x = ctx.sb("m16x", [128, 128], F32)
    pw_r = ctx.sb("pw_r", [128, 16 * 8], F32)
    a2_r = ctx.sb("a2_r", [128, 16 * 8], BF16)
    pfx_r = ctx.sb("pfx_r", [128, 16 * 8], F32)
    cw_r = ctx.sb("cw_r", [128, 16 * 8], F32)
    m8 = ctx.sb("m8_3c", [128, 8], F32)
    m8b = ctx.sb("m8b_3c", [128, 8], F32)
    m1c = ctx.sb("m1_3c", [128, 1], F32)
    m2c = ctx.sb("m2_3c", [128, 1], F32)
    row8 = ctx.sb("row8", [1, 8], F32)
    gsc = {k: ctx.sb(k + "_3c", [128, 1], F32) for k in ("mu", "varsum", "rstd", "t1", "t2")}
    gsc["cen"] = ctx.sb("cen_3c", [128, DM], F32)
    gsc["sq"] = ctx.sb("sq_3c", [128, DM], F32)
    psP = ctx.ps("psP3c", [128, 3584], F32)
    psT = ctx.ps("psT3c", [128, 128], BF16)

    def bank(k):
        return psP[:, k * 512:(k + 1) * 512]

    xn3 = A3.ap().bitcast(BF16)

    def _early_exit():
        ctx.wait(sy, "p", ctx.n("p"))
        ctx.inc(sy.dma_start(out=g3_o[0:128, :], in_=lnt[:, :]), "gst", 16)
        ctx.wait(sy, "gst")
        ctx.drain()
        ctx.close()

    ctx.constcol(1e-5)
    preload_set(nc, "sqrt_and_others")
    ctx.inc(sy.dma_start(out=idt[:, :], in_=ident[:, :]), "wld", 16)
    ctx.inc(sy.dma_start(out=tri[:, :], in_=tri_i[:, :]), "wld", 16)
    ctx.inc(sy.dma_start(out=wsel[:, :], in_=wsel_i[:, :]), "wld", 16)
    for i, src in enumerate((ln1wb, ln1bb, ln2wb, ln2bb)):
        ctx.inc(sy.dma_start(out=lnw[i][:, :], in_=src[:, :]), "wld", 16)
    for kt in range(8):
        ctx.inc(gp.dma_start(out=qkw[:, kt * 1536:(kt + 1) * 1536], in_=qkvT[kt * 128:(kt + 1) * 128, :]), "gld", 16)
        ctx.inc(gp.dma_start(out=rww[:, kt * 8:(kt + 1) * 8], in_=rw[kt * 128:(kt + 1) * 128, :]), "gld", 16)
    v.memset(onescol[:, :], 1.0)
    ctx.wait(sy, "wld")
    ctx.wait(gp, "gld")
    ctx.bar()

    for tt in range(NT):
        t0 = tt * 128
        ctx.wait(sy, "p", ctx.n("p"))
        ctx.inc(sy.dma_start(out=xn3[:, tt * DM:(tt + 1) * DM], in_=xn_i[t0:t0 + 128, :]), "xld", 16)
        ctx.wait(v, "xld", ctx.n("xld"))
        ctx.wait(a, "xld", ctx.n("xld"))
        ctx.ch(a)(a.activation(out=gsc["cen"][:, :], in_=xn3[:, tt * DM:(tt + 1) * DM], func=AF.Identity))
        n = emit_ln(ctx, gsc["cen"], 128, DM, lnt, gsc, eps=1e-5)
        ctx.ch(v)(v.tensor_tensor(gsc["sq"][:, :], lnt[:, :], lnw[0][:, :], op=ALU.mult))
        n = ctx.ch(v)(v.tensor_tensor(lnt[:, :], gsc["sq"][:, :], lnw[1][:, :], op=ALU.add))
        ctx.wait(te, "p", n)
        emit_transposes(ctx, lnt, 8, psT, A2, idt, dst_off=tt * 128, blk_stride=SQ)
        ctx.wait(te, "tpe", ctx.n("tpe"))
        for oc in range(3):
            ps = bank(oc % 2)
            ctx.wait(te, "evq", ctx.n("evq") - 1)
            for kt in range(8):
                mm = te.matmul(ps, A2[:, kt * SQ + tt * 128: kt * SQ + (tt + 1) * 128],
                               qkw[:, kt * 1536 + oc * 512: kt * 1536 + (oc + 1) * 512],
                               start=(kt == 0), stop=(kt == 7))
            ctx.inc(mm, "mmq")
            ctx.wait(v, "mmq", ctx.n("mmq"))
            ctx.inc(v.tensor_copy(A1[:, tt * 1536 + oc * 512: tt * 1536 + (oc + 1) * 512], ps), "evq")
    ctx.bar()
    if STAGE <= 1:
        _early_exit()
        return nc

    preload_set(nc, "exp_and_others")

    def qoff(tile, h):
        return tile * 1536 + (h // 4) * 384 + (h % 4) * 64

    def koff(tile, g):
        return tile * 1536 + g * 384 + 256

    def voff(tile, g):
        return tile * 1536 + g * 384 + 320

    aot = A2
    aoT = lnt
    for bh in range(2):
        ctx.wait(v, "p", ctx.n("p"))
        ctx.ch(v)(v.memset(sc_f[:, :], -1e30))
        pr3 = gsc["cen"].ap()[:, 0:256].rearrange("p (h d) -> p h d", d=64)
        scv = sc_f.ap().rearrange("p (h r) -> p h r", r=64)
        for qs in range(8):
            for ts in range(qs + 1):
                r = qs * 8 + ts
                for g in range(4):
                    q0 = qoff(qs * 2 + bh, 4 * g)
                    qb = A1.ap()[:, q0:q0 + 256].rearrange("p (h d) -> p h d", d=64)
                    kb = A1.ap()[:, koff(ts * 2 + bh, g): koff(ts * 2 + bh, g) + 64].unsqueeze(1).broadcast_to([128, 4, 64])
                    ctx.ch(v)(v.tensor_tensor(pr3, qb, kb, op=ALU.mult))
                    ctx.ch(v)(v.reduce_sum(scv[:, 4 * g:4 * g + 4, r:r + 1], pr3, axis=AX.X))
        sc3 = sc_f.ap().rearrange("p (hq t) -> p hq t", t=8)
        ctx.ch(v)(v.tensor_reduce(m16x[:, 0:128], sc3, axis=AX.X, op=ALU.max))
        ctx.ch(v)(v.tensor_scalar_mul(m16x[:, 0:128], m16x[:, 0:128], -1.0))
        ctx.ch(v)(v.tensor_tensor(sc3, sc3, m16x[:, 0:128].unsqueeze(2).broadcast_to([128, 128, 8]), op=ALU.add))
        ctx.ch(a)(a.activation(out=sc_f[:, :], in_=sc_f[:, :], func=AF.Exp, scale=0.125))
        ctx.ch(v)(v.reduce_sum(m16x[:, 0:128], sc3, axis=AX.X))
        ctx.ch(v)(v.reciprocal(m16x[:, 0:128], m16x[:, 0:128]))
        ctx.ch(v)(v.tensor_tensor(sc3, sc3, m16x[:, 0:128].unsqueeze(2).broadcast_to([128, 128, 8]), op=ALU.mult))
        for qs in range(8):
            tt = qs * 2 + bh
            ctx.ch(v)(v.memset(aot[:, tt * DM:(tt + 1) * DM], 0.0))
            for h in range(16):
                for ts in range(qs + 1):
                    col = h * 64 + qs * 8 + ts
                    ctx.ch(v)(v.scalar_tensor_tensor(aot[:, tt * DM + h * 64: tt * DM + (h + 1) * 64],
                                                     A1[:, voff(ts * 2 + bh, h // 4): voff(ts * 2 + bh, h // 4) + 64],
                                                     sc_f[:, col:col + 1],
                                                     aot[:, tt * DM + h * 64: tt * DM + (h + 1) * 64],
                                                     op0=ALU.mult, op1=ALU.add))
    ctx.bar()
    if STAGE <= 2:
        _early_exit()
        return nc

    for kt in range(8):
        ctx.inc(gp.dma_start(out=pjw[:, kt * DM:(kt + 1) * DM], in_=projT[kt * 128:(kt + 1) * 128, :]), "gld3", 16)
    ctx.wait(te, "gld3", ctx.n("gld3"))
    for tt in range(NT):
        n0 = ctx.n("p")
        ctx.wait(te, "p", n0)
        emit_transposes(ctx, aot[:, tt * DM:(tt + 1) * DM], 8, psT, aoT, idt)
        ctx.wait(te, "tpe", ctx.n("tpe"))
        for oc in range(2):
            ps = bank(2 + oc)
            for kt in range(8):
                mm = te.matmul(ps, aoT[:, kt * 128:(kt + 1) * 128],
                               pjw[:, kt * DM + oc * 512: kt * DM + (oc + 1) * 512],
                               start=(kt == 0), stop=(kt == 7))
            ctx.inc(mm, "mmp")
        ctx.wait(v, "mmp", ctx.n("mmp"))
        ctx.ch(v)(v.tensor_tensor(h1t[:, tt * DM: tt * DM + 512], bank(2), xn3[:, tt * DM: tt * DM + 512], op=ALU.add))
        ctx.ch(v)(v.tensor_tensor(h1t[:, tt * DM + 512: (tt + 1) * DM], bank(3), xn3[:, tt * DM + 512:(tt + 1) * DM], op=ALU.add))
    ctx.bar()
    if STAGE <= 3:
        _early_exit()
        return nc

    preload_set(nc, "sqrt_and_others")
    for tt in range(NT):
        s_idx = tt // 2
        bh = tt % 2
        ctx.wait(a, "p", ctx.n("p"))
        ctx.ch(a)(a.activation(out=gsc["cen"][:, :], in_=h1t[:, tt * DM:(tt + 1) * DM], func=AF.Identity))
        n = emit_ln(ctx, gsc["cen"], 128, DM, lnt, gsc, eps=1e-5)
        ctx.ch(v)(v.tensor_tensor(gsc["sq"][:, :], lnt[:, :], lnw[2][:, :], op=ALU.mult))
        n = ctx.ch(v)(v.tensor_tensor(lnt[:, :], gsc["sq"][:, :], lnw[3][:, :], op=ALU.add))
        ctx.wait(te, "p", n)
        emit_transposes(ctx, lnt, 8, psT, A2, idt, dst_off=tt * 128, blk_stride=SQ)
    ctx.bar()
    if STAGE <= 35:
        _early_exit()
        return nc
    preload_set(nc, "exp_and_others")
    for tt in range(NT if STAGE >= 38 else (STAGE - 35)):
        s_idx = tt // 2
        bh = tt % 2
        ctx.wait(te, "mmc", ctx.n("mmc"))
        ctx.wait(te, "p", ctx.n("p"))
        for kt in range(8):
            mm = te.matmul(psP[:, 3072:3080], A2[:, kt * SQ + tt * 128: kt * SQ + (tt + 1) * 128],
                           rww[:, kt * 8:(kt + 1) * 8], start=(kt == 0), stop=(kt == 7))
        ctx.inc(mm, "mmr")
        ctx.wait(v, "mmr", ctx.n("mmr"))
        ctx.ch(v)(v.tensor_reduce(m1c[:, 0:1], psP[:, 3072:3080], axis=AX.X, op=ALU.max))
        ctx.ch(v)(v.tensor_scalar_mul(m1c[:, 0:1], m1c[:, 0:1], -1.0))
        ctx.ch(v)(v.tensor_scalar(m8[:, :], psP[:, 3072:3080], m1c[:, 0:1], None, op0=ALU.add))
        nex = ctx.n("p")
        ctx.wait(a, "p", nex)
        ctx.ch(a)(a.activation(out=m8[:, :], in_=m8[:, :], func=AF.Exp))
        ctx.ch(v)(v.reduce_sum(m1c[:, 0:1], m8[:, :], axis=AX.X))
        ctx.ch(v)(v.reciprocal(m1c[:, 0:1], m1c[:, 0:1]))
        ctx.ch(v)(v.tensor_scalar(m8[:, :], m8[:, :], m1c[:, 0:1], None, op0=ALU.mult))
        ctx.ch(v)(v.tensor_reduce(m1c[:, 0:1], m8[:, :], axis=AX.X, op=ALU.max))
        ctx.ch(v)(v.tensor_scalar(m8b[:, :], m8[:, :], m1c[:, 0:1], None, op0=ALU.is_equal))
        ctx.ch(v)(v.scalar_tensor_tensor(gsc["sq"][:, 0:8], m8b[:, :], -1e30, m8[:, :], op0=ALU.mult, op1=ALU.add))
        ctx.ch(v)(v.tensor_reduce(m2c[:, 0:1], gsc["sq"][:, 0:8], axis=AX.X, op=ALU.max))
        ctx.ch(v)(v.tensor_scalar(gsc["sq"][:, 0:8], gsc["sq"][:, 0:8], m2c[:, 0:1], None, op0=ALU.is_equal))
        ctx.ch(v)(v.tensor_add(gsc["t1"][:, 0:1], m1c[:, 0:1], m2c[:, 0:1]))
        ctx.ch(v)(v.reciprocal(gsc["t1"][:, 0:1], gsc["t1"][:, 0:1]))
        ctx.ch(v)(v.tensor_mul(m1c[:, 0:1], m1c[:, 0:1], gsc["t1"][:, 0:1]))
        ctx.ch(v)(v.tensor_mul(m2c[:, 0:1], m2c[:, 0:1], gsc["t1"][:, 0:1]))
        ctx.ch(v)(v.tensor_scalar(m8[:, :], m8b[:, :], m1c[:, 0:1], None, op0=ALU.mult))
        ctx.ch(v)(v.scalar_tensor_tensor(pw_r[:, tt * 8:(tt + 1) * 8], gsc["sq"][:, 0:8], m2c[:, 0:1], m8[:, :], op0=ALU.mult, op1=ALU.add))
        nah = ctx.ch(v)(v.tensor_tensor(a2_r[:, tt * 8:(tt + 1) * 8], m8b[:, :], gsc["sq"][:, 0:8], op=ALU.add))
        ctx.wait(te, "p", nah)
        mm = te.matmul(psP[0:1, 2560:2568], onescol[:, 0:1],
                       a2_r[:, tt * 8:(tt + 1) * 8], start=True, stop=True)
        ctx.inc(mm, "mmc")
        mm = te.matmul(psP[:, 3136:3144], tri[:, :], a2_r[:, tt * 8:(tt + 1) * 8], start=True, stop=True)
        ctx.inc(mm, "mmc")
        ctx.wait(v, "mmc", ctx.n("mmc"))
        ctx.wait(v, "cst", ctx.n("cst"))
        ctx.ch(v)(v.tensor_copy(pfx_r[:, tt * 8:(tt + 1) * 8], psP[:, 3136:3144]))
        if bh == 0:
            nc0 = ctx.ch(v)(v.tensor_copy(row8[0:1, :], psP[0:1, 2560:2568]))
        else:
            nc0 = ctx.ch(v)(v.tensor_tensor(row8[0:1, :], row8[0:1, :],
                                            psP[0:1, 2560:2568], op=ALU.add))
        ctx.wait(sy, "p", nc0)
        if bh == 0:
            ctx.inc(sy.dma_start(out=cnt0_d[s_idx:s_idx + 1, :], in_=row8[0:1, :]), "cst", 16)
        else:
            ctx.inc(sy.dma_start(out=mc_d[s_idx:s_idx + 1, :], in_=row8[0:1, :]), "cst", 16)
        ctx.wait(sy, "cst")
    ctx.bar()
    if STAGE <= 4:
        _early_exit()
        return nc

    ctx.inc(gp.collective_compute("AllGather", ALU.bypass, [[0, 1, 2, 3, 4, 5, 6, 7]],
                                  ins=[mc_d.opt()], outs=[gall_d.opt()]), "cc", 1)
    ctx.wait(gp, "cc")
    ctx.bar()
    gsb = ctx.sb("gsb", [64, 8], F32)
    ctx.inc(sy.dma_start(out=gsb[:, :], in_=gall_d[:, :]), "gld2", 16)
    ctx.wait(te, "gld2", ctx.n("gld2"))
    mm = te.matmul(psP[0:8, 3168:3176], wsel[:, :], gsb[:, :], start=True, stop=True)
    ctx.inc(mm, "mmo")
    ctx.wait(v, "mmo", ctx.n("mmo"))
    nof = ctx.ch(v)(v.tensor_copy(m8[0:8, :], psP[0:8, 3168:3176]))
    ctx.wait(sy, "p", nof)
    ctx.inc(sy.dma_start(out=offs_d[:, :], in_=m8[0:8, :]), "ost", 16)
    ctx.wait(sy, "ost")
    ctx.bar()
    if STAGE <= 5:
        _early_exit()
        return nc
    for tt in range(NT):
        s_idx = tt // 2
        bh = tt % 2
        ctx.wait(sy, "p", ctx.n("p"))
        ctx.inc(sy.dma_start(out=m8[:, :], in_=offs_d[s_idx:s_idx + 1, :].broadcast_to([128, 8])), "obl", 16)
        if bh == 1:
            ctx.inc(sy.dma_start(out=m8b[:, :], in_=cnt0_d[s_idx:s_idx + 1, :].broadcast_to([128, 8])), "obl", 16)
        ctx.wait(v, "obl", ctx.n("obl"))
        ctx.ch(v)(v.tensor_tensor(gsc["sq"][:, 0:8], pfx_r[:, tt * 8:(tt + 1) * 8], m8[:, :], op=ALU.add))
        if bh == 1:
            ctx.ch(v)(v.tensor_tensor(gsc["sq"][:, 0:8], gsc["sq"][:, 0:8], m8b[:, :], op=ALU.add))
        ctx.ch(v)(v.tensor_scalar(gsc["sq"][:, 0:8], gsc["sq"][:, 0:8], CAP, None, op0=ALU.is_lt))
        ctx.ch(v)(v.tensor_tensor(cw_r[:, tt * 8:(tt + 1) * 8], pw_r[:, tt * 8:(tt + 1) * 8], gsc["sq"][:, 0:8], op=ALU.mult))
    ctx.bar()
    if STAGE <= 6:
        _early_exit()
        return nc

    preload_set(nc, "sigmoid_and_others")
    w1e = A1[:, 0:16384]
    fc1g = A1[:, 16384:24576]
    gelt = gsc["cen"][:, 0:512]
    gel2 = gsc["sq"][:, 0:512]
    A3f = A3
    for th in range(2):
        ctx.wait(v, "mmd", ctx.n("mmd"))
        ctx.ch(v)(v.memset(A3f[:, :], 0.0))
        for e in range(8):
            ctx.wait(gp, "mmg", ctx.n("mmg"))
            ctx.wait(gp, "p", ctx.n("p"))
            for kt in range(8):
                ctx.inc(gp.dma_start(out=w1e[:, kt * FE:(kt + 1) * FE], in_=mw1[e, kt * 128:(kt + 1) * 128, :]), "ewl", 16)
            ctx.wait(te, "ewl", ctx.n("ewl"))
            for tc in range(2):
                c0 = th * 1024 + tc * 512
                gp1 = [0, 0]
                for ft in range(16):
                    fb = ft % 2
                    if ft >= 2:
                        ctx.wait(te, "p", gp1[-2])
                    for kt in range(8):
                        mm = te.matmul(bank(fb), w1e[:, kt * FE + ft * 128: kt * FE + (ft + 1) * 128],
                                       A2[:, kt * SQ + c0: kt * SQ + c0 + 512], start=(kt == 0), stop=(kt == 7))
                    ctx.inc(mm, "mmg")
                    ctx.wait(v, "mmg", ctx.n("mmg"))
                    ctx.ch(v)(v.tensor_copy(gelt[:, :], bank(fb)))
                    ctx.ch(v)(v.tensor_tensor(gel2[:, :], gelt[:, :], gelt[:, :], op=ALU.mult))
                    ctx.ch(v)(v.tensor_tensor(gel2[:, :], gel2[:, :], gelt[:, :], op=ALU.mult))
                    ctx.ch(v)(v.scalar_tensor_tensor(gel2[:, :], gel2[:, :], 0.044715, gelt[:, :], op0=ALU.mult, op1=ALU.add))
                    ctx.ch(a)(a.activation(out=gel2[:, :], in_=gel2[:, :], func=AF.Sigmoid, scale=1.5957691216057308))
                    nnn = ctx.ch(v)(v.tensor_tensor(fc1g[:, ft * 512:(ft + 1) * 512], gelt[:, :], gel2[:, :], op=ALU.mult))
                    gp1.append(nnn)
                for tsp in range(2):
                    ctx.wait(te, "p", ctx.n("p"))
                    for ft in range(16):
                        pb = ft % 2
                        ctx.wait(gp, "mmd", max(0, ctx.n("mmd") - 2))
                        ctx.inc(gp.dma_start(out=w2d[pb][:, :], in_=mw2[e, ft * 128:(ft + 1) * 128, :]), f"w2l{pb}", 16)
                        ctx.wait(te, f"w2l{pb}", ctx.n(f"w2l{pb}"))
                        for ts2 in range(2):
                            for oc in range(2):
                                mm = te.matmul(bank(2 + ts2 * 2 + oc),
                                               fc1g[:, ft * 512 + (tsp * 2 + ts2) * 128: ft * 512 + (tsp * 2 + ts2) * 128 + 128],
                                               w2d[pb][:, oc * 512:(oc + 1) * 512],
                                               start=(ft == 0), stop=(ft == 15))
                            ctx.inc(mm, "mmd")
                    ctx.wait(v, "mmd", ctx.n("mmd"))
                    for ts2 in range(2):
                        tloc = tc * 4 + tsp * 2 + ts2
                        tt = th * 8 + tloc
                        ttl = tloc + th * 0
                        for oc in range(2):
                            ctx.ch(v)(v.scalar_tensor_tensor(A3f[:, ttl * DM + oc * 512: ttl * DM + (oc + 1) * 512],
                                                             bank(2 + ts2 * 2 + oc), cw_r[:, tt * 8 + e: tt * 8 + e + 1],
                                                             A3f[:, ttl * DM + oc * 512: ttl * DM + (oc + 1) * 512],
                                                             op0=ALU.mult, op1=ALU.add))
                    nmd = ctx.n("p")
                    ctx.wait(te, "p", nmd)
        for ttl in range(8):
            tt = th * 8 + ttl
            t0 = tt * 128
            ctx.wait(v, "p", ctx.n("p"))
            ctx.wait(v, "gst", max(0, ctx.n("gst")))
            nnn = ctx.ch(v)(v.tensor_tensor(lnt[:, :], h1t[:, tt * DM:(tt + 1) * DM], A3f[:, ttl * DM:(ttl + 1) * DM], op=ALU.add))
            ctx.wait(sy, "p", nnn)
            ctx.inc(sy.dma_start(out=g3_o[t0:t0 + 128, :], in_=lnt[:, :]), "gst", 16)
            ctx.wait(sy, "gst")
    ctx.drain()
    ctx.close()
    return nc




def build_l3e():
    ctx = Ctx("l3e")
    nc = ctx.nc
    x3 = nc.dram_tensor("x3", [SQ, DM], F32, kind="ExternalInput").ap()
    hs3 = nc.dram_tensor("hs3", [SQ, DM], BF16, kind="ExternalInput").ap()
    xn3 = nc.dram_tensor("xn3", [SQ, DM], BF16, kind="ExternalInput").ap()
    g3 = nc.dram_tensor("g3", [SQ, DM], BF16, kind="ExternalInput").ap()
    gmb = nc.dram_tensor("gmb", [128, DM], F32, kind="ExternalInput").ap()
    btb = nc.dram_tensor("btb", [128, DM], F32, kind="ExternalInput").ap()
    x4_o = nc.dram_tensor("x4", [SQ, DM], F32, kind="ExternalOutput").ap()

    v, a, te, sy, gp = nc.vector, nc.scalar, nc.tensor, nc.sync, nc.gpsimd
    gm = ctx.sb("gm3e", [128, DM], F32)
    bt = ctx.sb("bt3e", [128, DM], F32)
    xts = [ctx.sb(f"xts3e{i}", [128, DM], F32) for i in range(2)]
    hss = [ctx.sb(f"hss3e{i}", [128, DM], BF16) for i in range(2)]
    xns = [ctx.sb(f"xns3e{i}", [128, DM], BF16) for i in range(2)]
    g3s = [ctx.sb(f"g3s3e{i}", [128, DM], BF16) for i in range(2)]
    yt = ctx.sb("yt3e", [128, DM], F32)
    s3t = ctx.sb("s3t3e", [128, DM], F32)
    gsl = ctx.sb("gsl3e", [128, DM], F32)
    x4s = ctx.sb("x4s3e", [128, DM], F32)
    scr = {k: ctx.sb(k + "_3e", [128, 1], F32) for k in ("mu", "varsum", "rstd", "t1", "t2")}
    scr["cen"] = ctx.sb("cen_3e", [128, DM], F32)
    scr["sq"] = ctx.sb("sq_3e", [128, DM], F32)

    ctx.constcol(1e-5)
    ctx.inc(sy.dma_start(out=gm[:, :], in_=gmb[:, :]), "wld", 16)
    ctx.inc(sy.dma_start(out=bt[:, :], in_=btb[:, :]), "wld", 16)
    for t, src in ((xts[0], x3), (hss[0], hs3), (xns[0], xn3), (g3s[0], g3)):
        ctx.inc(sy.dma_start(out=t[:, :], in_=src[0:128, :]), "xld0", 16)
    ctx.wait(sy, "wld")
    ctx.wait(sy, "xld0")
    ctx.bar()
    for it in range(NT):
        bi = it % 2
        t0 = it * 128
        if it + 1 < NT:
            for t, src in ((xts[1 - bi], x3), (hss[1 - bi], hs3), (xns[1 - bi], xn3), (g3s[1 - bi], g3)):
                ctx.inc(sy.dma_start(out=t[:, :], in_=src[t0 + 128:t0 + 256, :]), f"xld{1 - bi}", 16)
        ctx.wait(v, f"xld{bi}", 64 * (it // 2 + 1))
        ctx.wait(a, f"xld{bi}", 64 * (it // 2 + 1))
        ctx.ch(v)(v.tensor_tensor(yt[:, :], hss[bi][:, :], xns[bi][:, :], op=ALU.add))
        n = emit_ln(ctx, yt, 128, DM, s3t, scr, eps=1e-5)
        ctx.ch(v)(v.tensor_tensor(s3t[:, :], s3t[:, :], gm[:, :], op=ALU.mult))
        ctx.ch(v)(v.tensor_tensor(s3t[:, :], s3t[:, :], bt[:, :], op=ALU.add))
        ctx.ch(a)(a.activation(out=gsl[:, :], in_=g3s[bi][:, :], func=AF.Silu))
        ctx.wait(v, "st", ctx.n("st"))
        ctx.ch(v)(v.tensor_tensor(x4s[:, :], s3t[:, :], gsl[:, :], op=ALU.mult))
        nx4 = ctx.ch(v)(v.tensor_add(x4s[:, :], x4s[:, :], xts[bi][:, :]))
        ctx.wait(sy, "p", nx4)
        ctx.inc(sy.dma_start(out=x4_o[t0:t0 + 128, :], in_=x4s[:, :]), "st", 16)
        ctx.wait(sy, "st")
    ctx.drain()
    ctx.close()
    return nc


_IDENT = np.eye(128, dtype=np.float32).astype(BF)


def _run(nc, in_maps):
    res = run_bass_kernel_spmd(nc, in_maps, core_ids=list(range(NCORES)))
    return res.results


def _asc(a):
    return np.ascontiguousarray(a)


def _bcast(vec, rows=128):
    return _asc(np.broadcast_to(vec.reshape(1, -1), (rows, vec.size)))


def kernel(**ins):
    x0 = ins["L0_attn_hidden_states"]

    in_a = [{"x": _asc(x0[c]), "wq": ins["L0_attn_wq"], "wk": ins["L0_attn_wk"],
             "wv": ins["L0_attn_wv"], "ident": _IDENT} for c in range(NCORES)]
    ra = _run(build_l0a(), in_a)

    W1 = ins["L0_attn_W1"]
    W1st = np.zeros((128, 512), np.float32)
    for p in range(8):
        W1st[0:64, p * 64:(p + 1) * 64] = W1[2 * p]
        W1st[64:128, p * 64:(p + 1) * 64] = W1[2 * p + 1]
    alpha = ins["L0_attn_alpha"]
    const_b = {
        "W1st": W1st,
        "binit": _bcast(ins["L0_attn_b1"]),
        "gb": _bcast(ins["L0_attn_gamma"]),
        "beb": _bcast(ins["L0_attn_beta"]),
        "thb": _bcast(ins["L0_attn_theta"]),
        "tbb": _bcast(ins["L0_attn_theta_bias"]),
        "alpha2": _asc(np.concatenate([alpha, alpha]).reshape(128, 1)),
        "invar": _asc(np.tile(1.0 / np.arange(1, 65, dtype=np.float32), 2).reshape(128, 1)),
    }
    in_b = [dict(q=ra[c]["q"], k=ra[c]["k"], v=ra[c]["v"], qT=ra[c]["qT"],
                 kT=ra[c]["kT"], **const_b) for c in range(NCORES)]
    rb = _run(build_l0b(), in_b)

    wupT = _asc(ins["L0_ffn_w_up"].T)
    wdnT = _asc(ins["L0_ffn_w_down"].T)
    in_c = [{"x": _asc(x0[c]), "xnT": ra[c]["xnT"], "s_ffn": ra[c]["s_ffn"],
             "attn": rb[c]["attn"], "wo": ins["L0_attn_wo"], "wupT": wupT,
             "wdnT": wdnT, "ident": _IDENT} for c in range(NCORES)]
    rc = _run(build_l0c(), in_c)
    x1 = [rc[c]["x1"] for c in range(NCORES)]

    bq = _bcast(ins["L1_attn_bq"])
    in_1a = [{"x": x1[c], "Wq": ins["L1_attn_Wq"], "bqb": bq, "ident": _IDENT}
             for c in range(NCORES)]
    r1a = _run(build_l1a(), in_1a)

    xnTf = _asc(np.concatenate([r1a[c]["xnT"] for c in range(NCORES)], axis=1))
    wih = ins["L1_ffn_gru_w_ih"]
    ctxwT = _asc(ins["L1_ffn_ctx_w"].T)
    ctxb = _asc(ins["L1_ffn_ctx_b"].reshape(DM, 1))
    in_1b = []
    for c in range(NCORES):
        wihT_c = _asc(np.concatenate(
            [wih[g * 1024 + c * 128: g * 1024 + (c + 1) * 128].T for g in range(3)], axis=1))
        in_1b.append({"xnTf": xnTf, "wihTc": wihT_c, "ctxwT": ctxwT, "ctxb": ctxb})
    r1b = _run(build_l1b(), in_1b)

    xw_full = np.zeros((16384, 3072), BF)
    for c in range(NCORES):
        xwc = r1b[c]["xw"]
        for g in range(3):
            xw_full[:, g * 1024 + c * 128: g * 1024 + (c + 1) * 128] = xwc[:, g * 128:(g + 1) * 128]
    xwp = np.zeros((258, 8, 8 * 3072), BF)
    tmp = xw_full.reshape(8, 2048, 3072).swapaxes(0, 1)
    xwp[:256] = tmp.reshape(256, 8, 8, 3072).transpose(0, 2, 1, 3).reshape(256, 8, 8 * 3072)
    h0T = r1b[0]["h0T"]
    hT0 = _asc(h0T.reshape(8, 128, 8).transpose(1, 0, 2).reshape(128, 64))
    h0r = _asc(h0T.T)
    whhT = _asc(ins["L1_ffn_gru_w_hh"].T)
    i8 = _asc(np.eye(8, dtype=np.float32).astype(BF))
    in_1c = []
    for c in range(NCORES):
        bm = np.zeros((8, 1), BF)
        bm[c] = 1.0
        in_1c.append({"xwp": xwp, "hT0": hT0, "h0r": h0r, "whhT": whhT,
                      "i8b": i8, "bmask": bm})
    r1c = _run(build_l1c(), in_1c)

    exprT = _asc(ins["L1_ffn_expr_w"].T)
    eyef = _asc(np.broadcast_to(np.eye(8, dtype=np.float32).reshape(1, 64), (128, 64)))
    in_1d = [{"hs": _asc(r1c[c]["hs"].reshape(SQ, DM)), "xnT": r1a[c]["xnT"],
              "exprT": exprT, "eyef": eyef} for c in range(NCORES)]
    r1d = _run(build_l1d(), in_1d)

    pen8 = _asc(np.concatenate([r1d[c]["pen"] for c in range(NCORES)], axis=1))
    gwT = np.zeros((9, DM, 2048), np.float32)
    uwT = np.zeros((9, DM, 2048), np.float32)
    dwT = np.zeros((9, 2048, DM), np.float32)
    gwT[:8] = np.swapaxes(ins["L1_ffn_gate_w"], 1, 2)
    gwT[8] = ins["L1_ffn_shared_gate_w"].T
    uwT[:8] = np.swapaxes(ins["L1_ffn_up_w"], 1, 2)
    uwT[8] = ins["L1_ffn_shared_up_w"].T
    dwT[:8] = np.swapaxes(ins["L1_ffn_down_w"], 1, 2)
    dwT[8] = ins["L1_ffn_shared_down_w"].T
    in_1e = [{"x1": x1[c], "s1": r1a[c]["s1"], "xnT": r1a[c]["xnT"],
              "cos": r1d[c]["cos"], "pen8": pen8, "gwT": gwT, "uwT": uwT,
              "dwT": dwT} for c in range(NCORES)]
    r1e = _run(build_l1e(), in_1e)
    x2 = [r1e[c]["x2"] for c in range(NCORES)]

    wT2 = {nm: _asc(ins[k].T) for nm, k in (("WkT", "L2_attn_Wk"), ("WqT", "L2_attn_Wq"),
                                            ("WvT", "L2_attn_Wv"), ("WaT", "L2_attn_Wa"),
                                            ("WbT", "L2_attn_Wb"))}
    in_2a = [dict(x=x2[c], ident=_IDENT, **wT2) for c in range(NCORES)]
    r2a = _run(build_l2a(), in_2a)

    def colform(sv):
        return _asc(sv.reshape(sv.size // 128, 128).T)

    const_2b = {
        "w1T": _asc(ins["L2_ffn_w1_w"].T), "w2T": _asc(ins["L2_ffn_w2_w"].T),
        "h1T": _asc(ins["L2_ffn_h1_w"].T), "h2T": _asc(ins["L2_ffn_h2_w"].T),
        "w3T": _asc(ins["L2_ffn_w3_w"].T),
        "s1c": colform(ins["L2_ffn_w1_s"]), "s2c": colform(ins["L2_ffn_w2_s"]),
        "h1c": colform(ins["L2_ffn_h1_s"]), "h2c": colform(ins["L2_ffn_h2_s"]),
        "s3b": _bcast(ins["L2_ffn_w3_s"]),
    }
    in_2b = [dict(x=x2[c], s2=r2a[c]["s2"], xnT=r2a[c]["xnT"], **const_2b)
             for c in range(NCORES)]
    r2b = _run(build_l2b(), in_2b)
    x3 = [r2b[c]["x3"] for c in range(NCORES)]

    W4x = _asc(np.concatenate([ins["L3_attn_Wg"][:1024], ins["L3_attn_Wi"][:1024],
                               ins["L3_attn_Wf"][:1024], ins["L3_attn_Wo"][:1024]], axis=1))
    b4 = np.concatenate([ins["L3_attn_bg"], ins["L3_attn_bi"], ins["L3_attn_bf"],
                         ins["L3_attn_bo"]])
    in_3a = [{"x": x3[c], "W4x": W4x, "b4b": _bcast(b4), "ident": _IDENT}
             for c in range(NCORES)]
    r3a = _run(build_l3a(), in_3a)

    xg_all = np.stack([r3a[c]["xg"] for c in range(NCORES)])
    xgp = np.zeros((514, 8, 4 * 4096), BF)
    tmp3 = xg_all.swapaxes(0, 1)
    xgp[:512] = tmp3.reshape(512, 4, 8, 4096).transpose(0, 2, 1, 3).reshape(512, 8, 4 * 4096)
    whh4T = _asc(np.concatenate([ins["L3_attn_Wg"][1024:], ins["L3_attn_Wi"][1024:],
                                 ins["L3_attn_Wf"][1024:], ins["L3_attn_Wo"][1024:]], axis=1))
    begb = _asc(np.broadcast_to(ins["L3_attn_beg"].reshape(1, DM), (8, DM)))
    in_3b = []
    for c in range(NCORES):
        bm = np.zeros((8, 1), BF)
        bm[c] = 1.0
        in_3b.append({"xgp": xgp, "whh4T": whh4T, "wegN": ins["L3_attn_Weg"],
                      "begb": begb, "i8b": i8, "bmask": bm})
    r3b = _run(build_l3b(), in_3b)

    xn3_all = np.stack([r3a[c]["xn"] for c in range(NCORES)])
    tri = _asc(np.triu(np.ones((128, 128), np.float32), 1).astype(BF))
    qkvT3 = _asc(ins["L3_ffn_qkv_weight"].T)
    projT3 = _asc(ins["L3_ffn_proj_weight"].T)
    const_3c = {
        "qkvT": qkvT3, "projT": projT3, "rw": ins["L3_ffn_router_weight"],
        "ln1wb": _bcast(ins["L3_ffn_ln1_weight"]), "ln1bb": _bcast(ins["L3_ffn_ln1_bias"]),
        "ln2wb": _bcast(ins["L3_ffn_ln2_weight"]), "ln2bb": _bcast(ins["L3_ffn_ln2_bias"]),
        "mw1": ins["L3_ffn_moe_w1"], "mw2": ins["L3_ffn_moe_w2"],
        "tri": tri, "ident": _IDENT,
    }
    in_3c = []
    for c in range(NCORES):
        wsel = np.zeros((64, 8), np.float32)
        for cp in range(8):
            for sp in range(8):
                for s in range(8):
                    if sp < s or (sp == s and cp < c):
                        wsel[8 * cp + sp, s] = 1.0
        in_3c.append(dict(xnsl=_asc(xn3_all[:, 256 * c:256 * (c + 1), :].reshape(SQ, DM)),
                          wsel=wsel, **const_3c))
    r3c = _run(build_l3c(), in_3c)

    gmb = _bcast(ins["L3_attn_gamma"])
    btb = _bcast(ins["L3_attn_beta"])
    in_3e = []
    for b in range(NCORES):
        g3b = _asc(np.concatenate([r3c[c]["g3"][b * 256:(b + 1) * 256] for c in range(NCORES)], axis=0))
        in_3e.append({"x3": x3[b], "hs3": _asc(r3b[b]["hs"].reshape(SQ, DM)),
                      "xn3": r3a[b]["xn"], "g3": g3b, "gmb": gmb, "btb": btb})
    r3e = _run(build_l3e(), in_3e)
    return np.stack([r3e[c]["x4"] for c in range(NCORES)])
